# version 21
# baseline (speedup 1.0000x reference)
import numpy as np
import ml_dtypes
import concourse.bass as bass
import concourse.mybir as mybir
from concourse.bass_utils import run_bass_kernel_spmd
from contextlib import ExitStack

F32 = mybir.dt.float32
BF16 = mybir.dt.bfloat16
I8 = mybir.dt.int8
AF = mybir.ActivationFunctionType
ALU = mybir.AluOpType
AX = mybir.AxisListType

MAGIC = 12582912.0
NT = 4096
NWARM = 512
NTW = NT + NWARM
DM = 2048
DFF = 8192
NS = 128
NH = DM // 64
NSSM = 2 * NS + NH
SUB = 128
NSUB = NT // SUB
NSUBW = NTW // SUB
NSUP = 2
TSUP = NT // NSUP
LN_EPS = 1e-5

ENGINES = ("sync", "scalar", "vector", "tensor", "gpsimd")


class Sem:
    __slots__ = ("h", "n", "dma")

    def __init__(self, handle, dma=False):
        self.h = handle
        self.n = 0
        self.dma = dma

    def inc(self, k):
        self.n += k
        return self.n


class Prog:

    def __init__(self, nc):
        self.nc = nc
        self.st = {k: [] for k in ENGINES}
        self.res = {}
        self.water = {k: {} for k in ENGINES}
        self.sems = []

    def em(self, eng, fn):
        self.st[eng].append(fn)

    def raw_wait(self, eng, sem, n):
        if sem.dma:
            n = sem.n
        if n <= 0:
            return
        w = self.water[eng]
        if w.get(sem, 0) >= n:
            return
        w[sem] = n
        self.em(eng, lambda e, s=sem.h, n=n: e.wait_ge(s, n))

    def _need(self, eng, reads, writes):
        waits = []
        for k in reads:
            r = self.res.get(k)
            if r and r["w"]:
                waits.append(r["w"])
        for k in writes:
            r = self.res.get(k)
            if r:
                if r["w"]:
                    waits.append(r["w"])
                waits.extend(r["r"])
        for sem, n in waits:
            self.raw_wait(eng, sem, n)

    def _record(self, mark, reads, writes):
        for k in writes:
            self.res[k] = {"w": mark, "r": []}
        for k in reads:
            r = self.res.setdefault(k, {"w": None, "r": []})
            r["r"].append(mark)

    def dma(self, eng, sem, out, in_, r=(), w=(), **kw):
        sem.dma = True
        self._need(eng, r, w)
        n = sem.inc(16)
        self.em(eng, lambda e, o=out, i=in_, s=sem.h, kw=kw:
                e.dma_start(out=o, in_=i, **kw).then_inc(s, 16))
        self._record((sem, n), r, w)
        return n

    def dmat(self, eng, sem, out, in_, r=(), w=()):
        sem.dma = True
        self._need(eng, r, w)
        n = sem.inc(16)
        self.em(eng, lambda e, o=out, i=in_, s=sem.h:
                e.dma_start_transpose(out=o, in_=i).then_inc(s, 16))
        self._record((sem, n), r, w)
        return n

    def op(self, eng, sem, fn, r=(), w=()):
        self._need(eng, r, w)
        n = sem.inc(1)
        self.em(eng, lambda e, fn=fn, s=sem.h: fn(e).then_inc(s, 1))
        self._record((sem, n), r, w)
        return n

    def op_nomark(self, eng, fn, r=(), w=()):
        self._need(eng, r, w)
        self.em(eng, fn)

    def fence(self, eng, keys):
        waits = []
        for k in keys:
            r = self.res.get(k)
            if r:
                if r["w"]:
                    waits.append(r["w"])
                waits.extend(r["r"])
        for sem, n in waits:
            self.raw_wait(eng, sem, n)

    def barrier(self):
        for eng in ENGINES:
            for s in self.sems:
                self.raw_wait(eng, s, s.n)
        self.res.clear()

    def replay(self):
        with self.nc.Block() as block:
            for eng in ENGINES:
                deco = getattr(block, eng)
                stream = self.st[eng]

                def make(stream):
                    def body(e):
                        for fn in stream:
                            fn(e)
                    return body
                deco(make(stream))


def K(*a):
    return tuple(a)


def build_program(nc):
    NK = DM // SUB
    NKF = DFF // SUB
    NSL1 = DFF // 512
    NSL2 = DM // 512
    W1CH = DFF // SUB
    W2CH = DM // SUB
    NCHUNK = NTW // 512
    P4SLAB = NTW // 4
    NELW = DFF * DM

    def din(name, shape, dt=F32):
        return nc.dram_tensor(name, list(shape), dt, kind="ExternalInput").ap()

    def dscr(name, shape, dt):
        return nc.dram_tensor(name, list(shape), dt, kind="Internal").ap()

    x_in = din("x", (NTW, DM))
    wx_in = din("wx", (DM, NSSM))
    wg_in = din("wg", (DM, DM))
    wo_in = din("wo", (DM, DM))
    w1_in = din("w1", (DFF, DM))
    w2_in = din("w2", (DM, DFF))
    bg_in = din("bg", (1, DM))
    dp_in = din("dp_rep", (128, DM))
    at_in = din("a_t", (NS, NH))
    onesrow_in = din("ones_row", (1, 128), BF16)
    onescol_in = din("ones_col", (128, 1), BF16)
    onesrowf_in = din("ones_row_f", (1, 128))
    onescolf_in = din("ones_col_f", (128, 1))
    identb_in = din("ident_b", (128, 128), BF16)
    ident32f_in = din("ident32_f", (32, 32))
    woff_in = din("woff", (128, 1))
    sel_in = din("sel_h", (NH, NH * 128), BF16)

    out_d = nc.dram_tensor("out", [NT, DM], F32, kind="ExternalOutput").ap()

    xnbuf = dscr("xnbuf", (NTW, DM), BF16)
    q1buf = dscr("q1buf", (NT, DM), BF16)
    ssmbuf = dscr("ssmbuf", (NTW, NSSM), BF16)
    hbuf = dscr("hbuf", (TSUP, DFF), F32)
    gbuf = dscr("gbuf", (NT, DM), F32)
    yfbuf = dscr("yfbuf", (NT, DM), BF16)
    t1q = dscr("t1q", (DFF, DM), BF16)
    t2q = dscr("t2q", (DM, DFF), BF16)
    ybuf = dscr("ybuf", (NH, NT), F32)
    q2buf = dscr("q2buf", (TSUP, DFF), BF16)
    t1b = dscr("t1b", (NSL1, 128, NK, 512), I8)
    t2b = dscr("t2b", (NSL2, 128, NKF, 512), I8)

    p = Prog(nc)
    with ExitStack() as ctx:
        def sb(name, shape, dt=F32, c=ctx):
            return c.enter_context(nc.sbuf_tensor(name, list(shape), dt))

        def ps(name, shape, dt=F32, c=ctx):
            return c.enter_context(nc.psum_tensor(name, list(shape), dt))

        def sem(name):
            s = Sem(ctx.enter_context(nc.semaphore(name)))
            p.sems.append(s)
            return s

        eV, eT, eA, eG = sem("eV"), sem("eT"), sem("eA"), sem("eG")
        ESEM = {"vector": eV, "tensor": eT, "scalar": eA, "gpsimd": eG}

        def cop(eng, fn, r=(), w=()):
            return p.op(eng, ESEM[eng], fn, r=r, w=w)

        def copq(eng, fn, r=(), w=()):
            p.op_nomark(eng, fn, r=r, w=w)

        c_ones_row = sb("c_ones_row", (1, 128), BF16)
        c_ones_col = sb("c_ones_col", (128, 1), BF16)
        c_ones_rowf = sb("c_ones_rowf", (1, 128))
        c_ones_colf = sb("c_ones_colf", (128, 1))
        c_identb = sb("c_identb", (128, 128), BF16)
        c_ident32f = sb("c_ident32f", (32, 32))
        c_at = sb("c_at", (NS, NH))
        c_woff = sb("c_woff", (128, 1))
        c_bgb = sb("c_bgb", (1, DM), BF16)
        c_eps = sb("c_eps", (128, 1))
        c_negmag = sb("c_negmag", (128, 1))
        c_sel = sb("c_sel", (NH, NH * 128), BF16)
        c_mean12 = sb("c_mean12", (128, 2))
        c_ws12 = sb("c_ws12", (128, 2))
        evs1_all = sb("evs1_all", (128, NSUB))
        amax1_all = sb("amax1_all", (128, NSUB))
        amax2_all = sb("amax2_all", (128, TSUP // SUB))
        evs2_all = sb("evs2_all", (128, TSUP // SUB))
        y_all = sb("y_all", (128, NSUB * NH))

        sC, sCg = sem("sC"), sem("sCg")
        p.dma("sync", sC, c_ones_row[:], onesrow_in, w=[K("c", 1)])
        p.dma("sync", sC, c_ones_col[:], onescol_in, w=[K("c", 2)])
        p.dma("sync", sC, c_ones_rowf[:], onesrowf_in, w=[K("c", 3)])
        p.dma("sync", sC, c_ones_colf[:], onescolf_in, w=[K("c", 4)])
        p.dma("sync", sC, c_identb[:], identb_in, w=[K("c", 5)])
        p.dma("sync", sC, c_ident32f[:], ident32f_in, w=[K("c", 6)])
        p.dma("sync", sC, c_at[:], at_in, w=[K("c", 7)])
        p.dma("sync", sC, c_woff[:], woff_in, w=[K("c", 8)])
        p.dma("sync", sC, c_sel[:], sel_in, w=[K("c", 11)])
        p.dma("gpsimd", sCg, c_bgb[:], bg_in, w=[K("c", 10)])
        cop("vector", lambda e: e.memset(c_eps[:], LN_EPS), w=[K("c", "eps")])
        cop("vector", lambda e: e.memset(c_negmag[:], -MAGIC), w=[K("c", "nm")])
        with ExitStack() as c3:
            t_x = [sb(f"t_x{i}", (128, DM), c=c3) for i in range(2)]
            t_sq = sb("t_sq", (128, DM), c=c3)
            t_xn = [sb(f"t_xn{i}", (128, DM), c=c3) for i in range(2)]
            t_xnb = [sb(f"t_xnb{i}", (128, DM), BF16, c=c3) for i in range(2)]
            t_q1m = sb("t_q1m", (128, DM), c=c3)
            t_q1b = [sb(f"t_q1b{i}", (128, DM), BF16, c=c3) for i in range(2)]
            cl = [{nm: sb(f"cl{i}_" + nm, (128, 1), c=c3) for nm in
                   ("sum", "sq", "m", "m2", "v", "rstd", "negmr",
                    "amax", "am1c", "rec", "s1")} for i in range(2)]
            sX = [sem("sX0"), sem("sX1")]
            sXN = [sem("sXN0"), sem("sXN1")]
            sQ1 = [sem("sQ10"), sem("sQ11")]
            wb = [sb(f"wb{i}", (128, DFF // 2), BF16, c=c3) for i in range(2)]
            wacc = sb("wacc", (128, W1CH + 2 * W2CH), c=c3)
            rs2 = sb("rs2", (128, 2), c=c3)
            m12s = sb("m12s", (1, 2), c=c3)
            pm11 = ps("pm11", (1, 2), c=c3)
            pm128 = ps("pm128", (128, 2), c=c3)
            sWB = [sem("sWB0"), sem("sWB1")]

            def ln_subtile(b, keyx):
                clb_ = cl[b]
                cop("vector", lambda e, b=b, clb_=clb_: e.reduce_sum(
                    out=clb_["sum"][:], in_=t_x[b][:], axis=AX.X),
                    r=[keyx], w=[K("cl", b, "sum")])
                cop("scalar", lambda e, b=b, clb_=clb_: e.activation(
                    out=t_sq[:], in_=t_x[b][:], func=AF.Square,
                    accum_out=clb_["sq"][:]),
                    r=[keyx], w=[K("t_sq"), K("cl", b, "sq")])
                cop("vector", lambda e, clb_=clb_: e.tensor_scalar(
                    out=clb_["m"][:], in0=clb_["sum"][:], scalar1=1.0 / DM,
                    scalar2=None, op0=ALU.mult),
                    r=[K("cl", b, "sum")], w=[K("cl", b, "m")])
                cop("vector", lambda e, clb_=clb_: e.tensor_tensor(
                    out=clb_["m2"][:], in0=clb_["m"][:], in1=clb_["m"][:],
                    op=ALU.mult), r=[K("cl", b, "m")], w=[K("cl", b, "m2")])
                cop("vector", lambda e, clb_=clb_: e.tensor_scalar(
                    out=clb_["v"][:], in0=clb_["sq"][:], scalar1=1.0 / DM,
                    scalar2=None, op0=ALU.mult),
                    r=[K("cl", b, "sq")], w=[K("cl", b, "v")])
                cop("vector", lambda e, clb_=clb_: e.tensor_tensor(
                    out=clb_["v"][:], in0=clb_["v"][:], in1=clb_["m2"][:],
                    op=ALU.subtract), r=[K("cl", b, "m2"), K("cl", b, "v")],
                    w=[K("cl", b, "v")])
                cop("scalar", lambda e, clb_=clb_: e.activation(
                    out=clb_["rstd"][:], in_=clb_["v"][:], func=AF.Ln,
                    bias=c_eps[:]),
                    r=[K("cl", b, "v"), K("c", "eps")], w=[K("cl", b, "rstd")])
                cop("scalar", lambda e, clb_=clb_: e.activation(
                    out=clb_["rstd"][:], in_=clb_["rstd"][:], func=AF.Exp,
                    scale=-0.5), r=[K("cl", b, "rstd")], w=[K("cl", b, "rstd")])
                cop("vector", lambda e, clb_=clb_: e.tensor_scalar(
                    out=clb_["negmr"][:], in0=clb_["m"][:],
                    scalar1=clb_["rstd"][:],
                    scalar2=-1.0, op0=ALU.mult, op1=ALU.mult),
                    r=[K("cl", b, "m"), K("cl", b, "rstd")],
                    w=[K("cl", b, "negmr")])
                cop("scalar", lambda e, b=b, clb_=clb_: e.activation(
                    out=t_xn[b][:], in_=t_x[b][:], func=AF.Identity,
                    scale=clb_["rstd"][:], bias=clb_["negmr"][:]),
                    r=[keyx, K("cl", b, "rstd"), K("cl", b, "negmr")],
                    w=[K("t_xn", b)])

            def p3_subtile(i):
                buf = i % 2
                real = i >= (NSUBW - NSUB)
                ri = i - (NSUBW - NSUB)
                p.dma("gpsimd", sX[buf], t_x[buf][:],
                      x_in[i * 128:(i + 1) * 128, :], w=[K("t_x", buf)])
                ln_subtile(buf, K("t_x", buf))
                cop("scalar", lambda e, b=buf: e.activation(
                    out=t_xnb[b][:], in_=t_x[b][:], func=AF.Identity,
                    scale=cl[b]["rstd"][:], bias=cl[b]["negmr"][:]),
                    r=[K("t_x", buf), K("cl", buf, "rstd"),
                       K("cl", buf, "negmr")],
                    w=[K("t_xnb", buf)])
                p.dma("scalar", sXN[buf], xnbuf[i * 128:(i + 1) * 128, :],
                      t_xnb[buf][:], r=[K("t_xnb", buf)], w=[K("xnbuf", i)])
                if real:
                    b_ = buf
                    clb_ = cl[b_]
                    cop("vector", lambda e, b_=b_, clb_=clb_: e.reduce_max(
                        out=clb_["amax"][:], in_=t_xn[b_][:], axis=AX.X,
                        apply_absolute_value=True),
                        r=[K("t_xn", b_)], w=[K("cl", b_, "amax")])
                    cop("vector", lambda e, b_=b_, clb_=clb_: e.tensor_scalar(
                        out=clb_["am1c"][:], in0=clb_["amax"][:], scalar1=1e-5,
                        scalar2=None, op0=ALU.max),
                        r=[K("cl", b_, "amax")], w=[K("cl", b_, "am1c")])
                    cop("vector", lambda e, r_=ri, clb_=clb_: e.tensor_copy(
                        out=amax1_all[:, r_:r_ + 1], in_=clb_["am1c"][:]),
                        r=[K("cl", b_, "am1c")], w=[K("amax1", ri)])
                    cop("vector", lambda e, clb_=clb_: e.reciprocal(
                        out=clb_["rec"][:], in_=clb_["am1c"][:]),
                        r=[K("cl", b_, "am1c")], w=[K("cl", b_, "rec")])
                    cop("vector", lambda e, clb_=clb_: e.tensor_scalar(
                        out=clb_["s1"][:], in0=clb_["rec"][:], scalar1=127.0,
                        scalar2=None, op0=ALU.mult),
                        r=[K("cl", b_, "rec")], w=[K("cl", b_, "s1")])
                    cop("vector", lambda e, b_=b_, clb_=clb_: e.tensor_scalar(
                        out=t_q1m[:], in0=t_xn[b_][:],
                        scalar1=clb_["s1"][:],
                        scalar2=MAGIC, op0=ALU.mult, op1=ALU.add),
                        r=[K("t_xn", b_), K("cl", b_, "s1")],
                        w=[K("t_q1m")])
                    cop("vector", lambda e, b_=b_: e.tensor_scalar(
                        out=t_q1m[:], in0=t_q1m[:], scalar1=MAGIC,
                        scalar2=127.0, op0=ALU.subtract, op1=ALU.min),
                        r=[K("t_q1m")], w=[K("t_q1m")])
                    cop("vector", lambda e, b=buf: e.tensor_scalar(
                        out=t_q1b[b][:], in0=t_q1m[:], scalar1=-128.0,
                        scalar2=None, op0=ALU.max),
                        r=[K("t_q1m")], w=[K("t_q1b", buf)])
                    p.dma("scalar", sQ1[buf],
                          q1buf[ri * 128:(ri + 1) * 128, :],
                          t_q1b[buf][:], r=[K("t_q1b", buf)],
                          w=[K("q1buf", ri)])

            wjunk = sb("wjunk", (128, DFF // 2), c=c3)

            def p1_mean_chunk(c):
                b = c % 2
                if c < W1CH:
                    src = w1_in[c * 128:(c + 1) * 128, :]
                    width = DM
                else:
                    cc = c - W1CH
                    src = w2_in[(cc // 2) * 128:(cc // 2 + 1) * 128,
                                (cc % 2) * (DFF // 2):
                                (cc % 2 + 1) * (DFF // 2)]
                    width = DFF // 2
                p.dma("gpsimd", sWB[b], wb[b][:, 0:width], src, w=[K("wb", b)])
                cop("scalar", lambda e, b=b, c=c, width=width: e.activation(
                    out=wjunk[:, 0:width], in_=wb[b][:, 0:width], func=AF.Abs,
                    accum_out=wacc[:, c:c + 1]),
                    r=[K("wb", b)], w=[K("wjunk"), K("wacc", c)])

            def p1_reduce():
                cop("vector", lambda e: e.reduce_sum(
                    out=rs2[:, 0:1], in_=wacc[:, 0:W1CH], axis=AX.X),
                    r=[K("wacc", c) for c in range(W1CH)], w=[K("rs2", 0)])
                cop("vector", lambda e: e.reduce_sum(
                    out=rs2[:, 1:2], in_=wacc[:, W1CH:W1CH + 2 * W2CH],
                    axis=AX.X),
                    r=[K("wacc", W1CH + c) for c in range(2 * W2CH)],
                    w=[K("rs2", 1)])
                cop("tensor", lambda e: e.matmul(
                    out=pm11[:], lhsT=c_ones_colf[:], rhs=rs2[:],
                    start=True, stop=True),
                    r=[K("rs2", 0), K("rs2", 1), K("c", 4)], w=[K("pm11")])
                cop("vector", lambda e: e.tensor_scalar(
                    out=m12s[:], in0=pm11[:], scalar1=1.0 / NELW, scalar2=1e-5,
                    op0=ALU.mult, op1=ALU.max),
                    r=[K("pm11")], w=[K("m12s")])
                cop("tensor", lambda e: e.matmul(
                    out=pm128[:], lhsT=c_ones_rowf[:], rhs=m12s[:],
                    start=True, stop=True),
                    r=[K("m12s"), K("c", 3)], w=[K("pm128")])
                cop("vector", lambda e: e.tensor_copy(
                    out=c_mean12[:], in_=pm128[:]),
                    r=[K("pm128")], w=[K("cm12")])
                cop("vector", lambda e: e.reciprocal(
                    out=c_ws12[:], in_=c_mean12[:]),
                    r=[K("cm12")], w=[K("cws12")])

            NCH = W1CH + 2 * W2CH
            mc = 0
            for i in range(NSUBW):
                p3_subtile(i)
                want = (i + 1) * NCH // NSUBW
                while mc < want:
                    p1_mean_chunk(mc)
                    mc += 1
            while mc < NCH:
                p1_mean_chunk(mc)
                mc += 1
            p1_reduce()
            cop("vector", lambda e: e.tensor_scalar(
                out=evs1_all[:], in0=amax1_all[:], scalar1=c_mean12[:, 0:1],
                scalar2=1.0 / 127.0, op0=ALU.mult, op1=ALU.mult),
                r=[K("cm12")] + [K("amax1", r_) for r_ in range(NSUB)],
                w=[K("evs1")])
            p.barrier()

        with ExitStack() as c4:
            c_wxb = sb("c_wxb", (128, NK * NSSM), BF16, c=c4)
            db_T = sb("db_T", (NH, NTW), BF16, c=c4)
            for k in range(NK):
                p.dma("gpsimd", sCg, c_wxb[:, k * NSSM:(k + 1) * NSSM],
                      wx_in[k * 128:(k + 1) * 128, :], w=[K("c", "wx", k)])
            sxnt = sb("sxnt", (128, NK * P4SLAB), BF16, c=c4)
            t_ssm = [sb(f"t_ssm{i}", (128, NSSM), BF16, c=c4) for i in range(2)]
            pwx = [ps(f"pwx{i}", (128, NSSM), c=c4) for i in range(2)]
            pdb = ps("pdb", (NH, 128), BF16, c=c4)
            sT4 = sem("sT4")
            sM4 = sem("sM4")
            nsub_slab = P4SLAB // SUB
            for slab in range(4):
                for k in range(NK):
                    p.dmat("sync", sT4, sxnt[:, k * P4SLAB:(k + 1) * P4SLAB],
                           xnbuf[slab * P4SLAB:(slab + 1) * P4SLAB,
                                 k * 128:(k + 1) * 128],
                           r=[K("xnbuf", slab * nsub_slab + j)
                              for j in range(nsub_slab)],
                           w=[K("sxnt", k)])
                for j in range(nsub_slab):
                    i = slab * nsub_slab + j
                    pb = i % 2
                    for k in range(NK):
                        fn = (lambda e, k=k, j=j, pb=pb: e.matmul(
                            out=pwx[pb][:],
                            lhsT=sxnt[:, k * P4SLAB + j * 128:
                                      k * P4SLAB + (j + 1) * 128],
                            rhs=c_wxb[:, k * NSSM:(k + 1) * NSSM],
                            start=(k == 0), stop=(k == NK - 1)))
                        if k < NK - 1:
                            copq("tensor", fn,
                                 r=[K("sxnt", k), K("c", "wx", k)],
                                 w=([K("pwx", pb)] if k == 0 else []))
                        else:
                            cop("tensor", fn,
                                r=[K("sxnt", kk) for kk in range(NK)] +
                                  [K("c", "wx", kk) for kk in range(NK)],
                                w=[K("pwx", pb)])
                    if i < NWARM // SUB:
                        cop("vector", lambda e, pb=pb: e.tensor_scalar(
                            out=t_ssm[pb][:], in0=pwx[pb][:],
                            scalar1=c_woff[:], scalar2=None, op0=ALU.add),
                            r=[K("pwx", pb), K("c", 8)], w=[K("t_ssm", pb)])
                    else:
                        cop("scalar", lambda e, pb=pb: e.activation(
                            out=t_ssm[pb][:], in_=pwx[pb][:], func=AF.Identity),
                            r=[K("pwx", pb)], w=[K("t_ssm", pb)])
                    p.dma("scalar", sM4, ssmbuf[i * 128:(i + 1) * 128, :],
                          t_ssm[pb][:], r=[K("t_ssm", pb)], w=[K("ssm", i)])
                    cop("tensor", lambda e, pb=pb: e.transpose(
                        out=pdb[:], in_=t_ssm[pb][:, 2 * NS:2 * NS + NH],
                        identity=c_identb[:]),
                        r=[K("t_ssm", pb), K("c", 5)], w=[K("pdb")])
                    cop("scalar", lambda e, i=i: e.activation(
                        out=db_T[:, i * 128:(i + 1) * 128], in_=pdb[:],
                        func=AF.Identity),
                        r=[K("pdb")], w=[K("dbT", i)])

            bB = [sb(f"bB{i}", (128, 512), BF16, c=c4) for i in range(2)]
            fB = [sb(f"fB{i}", (128, 512), c=c4) for i in range(2)]
            bC = [sb(f"bC{i}", (128, 512), BF16, c=c4) for i in range(2)]
            fC = [sb(f"fC{i}", (128, 512), c=c4) for i in range(2)]
            t_zt = [sb(f"t_zt{i}", (128, 512), c=c4) for i in range(2)]
            t_e = [sb(f"t_e{i}", (128, 512), c=c4) for i in range(2)]
            t_d5 = [sb(f"t_d5{i}", (128, 512), c=c4) for i in range(2)]
            t_a5 = [sb(f"t_a5{i}", (128, 512), c=c4) for i in range(2)]
            t_u5 = [sb(f"t_u5{i}", (128, 512), c=c4) for i in range(2)]
            t_hs = [sb(f"t_hs{i}", (128, 512), c=c4) for i in range(2)]
            t_mb = [sb(f"t_mb{i}", (128, 512), BF16, c=c4) for i in range(2)]
            st8 = sb("st8", (128, NH), c=c4)
            t_yrow = [sb(f"t_yrow{i}", (1, 512), c=c4) for i in range(2)]
            t_yb32 = sb("t_yb32", (NH, 128), c=c4)
            pz = [ps(f"pz{i}", (128, 512), c=c4) for i in range(2)]
            py = [ps(f"py{i}", (1, 512), c=c4) for i in range(2)]
            pytr = ps("pytr", (128, NH), c=c4)
            sB5 = [sem("sB5a"), sem("sB5b")]
            sC5 = [sem("sC5a"), sem("sC5b")]
            sYR = [sem("sYR0"), sem("sYR1")]
            sYL = sem("sYL")
            for tc in range(NCHUNK):
                cb = tc % 2
                realc = tc >= NWARM // 512
                rc = tc - NWARM // 512
                p.dmat("sync", sB5[cb], bB[cb][:],
                       ssmbuf[tc * 512:(tc + 1) * 512, 0:NS],
                       r=[K("ssm", tc * 4 + j) for j in range(4)],
                       w=[K("bB", cb)])
                cop("vector", lambda e, cb=cb: e.tensor_copy(
                    out=fB[cb][:], in_=bB[cb][:]),
                    r=[K("bB", cb)], w=[K("fB", cb)])
                if realc:
                    p.dmat("sync", sC5[cb], bC[cb][:],
                           ssmbuf[tc * 512:(tc + 1) * 512, NS:2 * NS],
                           r=[K("ssm", tc * 4 + j) for j in range(4)],
                           w=[K("bC", cb)])
                    cop("vector", lambda e, cb=cb: e.tensor_copy(
                        out=fC[cb][:], in_=bC[cb][:]),
                        r=[K("bC", cb)], w=[K("fC", cb)])
                def scan_head(h, tc=tc, cb=cb):
                    pb = h % 2
                    cop("tensor", lambda e, h=h, tc=tc, pb=pb: e.matmul(
                        out=pz[pb][:], lhsT=c_sel[:, h * 128:(h + 1) * 128],
                        rhs=db_T[:, tc * 512:(tc + 1) * 512],
                        start=True, stop=True),
                        r=[K("dbT", tc * 4), K("dbT", tc * 4 + 1),
                           K("dbT", tc * 4 + 2), K("dbT", tc * 4 + 3),
                           K("c", 11)],
                        w=[K("pz", pb)])
                    cop("vector", lambda e, pb=pb, cb=cb: e.tensor_tensor(
                        out=t_zt[pb][:], in0=pz[pb][:], in1=fB[cb][:],
                        op=ALU.add),
                        r=[K("pz", pb), K("fB", cb)], w=[K("zt", pb)])
                    cop("scalar", lambda e, pb=pb: e.activation(
                        out=t_e[pb][:], in_=t_zt[pb][:], func=AF.Exp),
                        r=[K("zt", pb)], w=[K("te", pb)])
                    cop("scalar", lambda e, pb=pb: e.activation(
                        out=t_d5[pb][:], in_=t_e[pb][:], func=AF.Ln, bias=1.0),
                        r=[K("te", pb)], w=[K("td", pb)])
                    cop("scalar", lambda e, h=h, pb=pb: e.activation(
                        out=t_a5[pb][:], in_=t_d5[pb][:], func=AF.Exp,
                        scale=c_at[:, h:h + 1]),
                        r=[K("td", pb), K("c", 7)], w=[K("ta", pb)])
                    cop("scalar", lambda e, pb=pb: e.activation(
                        out=t_u5[pb][:], in_=t_d5[pb][:], func=AF.Square),
                        r=[K("td", pb)], w=[K("tu", pb)])

                def scan_tail(h, tc=tc, cb=cb, rc=rc, realc=realc):
                    pb = h % 2
                    if tc == 0:
                        cop("vector", lambda e, pb=pb: e.tensor_tensor_scan(
                            out=t_hs[pb][:], data0=t_a5[pb][:],
                            data1=t_u5[pb][:],
                            initial=0.0, op0=ALU.mult, op1=ALU.add),
                            r=[K("ta", pb), K("tu", pb)], w=[K("hs", pb)])
                    else:
                        cop("vector", lambda e, h=h, pb=pb:
                            e.tensor_tensor_scan(
                                out=t_hs[pb][:], data0=t_a5[pb][:],
                                data1=t_u5[pb][:],
                                initial=st8[:, h:h + 1], op0=ALU.mult,
                                op1=ALU.add),
                            r=[K("ta", pb), K("tu", pb), K("st8", h)],
                            w=[K("hs", pb)])
                    cop("vector", lambda e, h=h, pb=pb: e.tensor_copy(
                        out=st8[:, h:h + 1], in_=t_hs[pb][:, 511:512]),
                        r=[K("hs", pb)], w=[K("st8", h)])
                    if realc:
                        cop("vector", lambda e, pb=pb, cb=cb: e.tensor_tensor(
                            out=t_mb[pb][:], in0=t_hs[pb][:], in1=fC[cb][:],
                            op=ALU.mult),
                            r=[K("hs", pb), K("fC", cb)], w=[K("mb", pb)])
                        cop("tensor", lambda e, pb=pb: e.matmul(
                            out=py[pb][:], lhsT=c_ones_col[:],
                            rhs=t_mb[pb][:],
                            start=True, stop=True),
                            r=[K("mb", pb), K("c", 2)], w=[K("py", pb)])
                        yb = h % 2
                        cop("vector", lambda e, pb=pb, yb=yb: e.tensor_copy(
                            out=t_yrow[yb][:], in_=py[pb][:]),
                            r=[K("py", pb)], w=[K("t_yrow", yb)])
                        p.dma("gpsimd", sYR[yb],
                              ybuf[h:h + 1, rc * 512:(rc + 1) * 512],
                              t_yrow[yb][:], r=[K("t_yrow", yb)],
                              w=[K("ybuf", rc, h)])

                for h in range(NH):
                    scan_head(h)
                    if h >= 1:
                        scan_tail(h - 1)
                scan_tail(NH - 1)
            for s_ in range(NSUB):
                p.dma("gpsimd", sYL, t_yb32[:],
                      ybuf[:, s_ * 128:(s_ + 1) * 128],
                      r=[K("ybuf", s_ // 4, h) for h in range(NH)],
                      w=[K("t_yb32")])
                cop("tensor", lambda e: e.transpose(
                    out=pytr[:], in_=t_yb32[:],
                    identity=c_ident32f[0:NH, 0:NH]),
                    r=[K("t_yb32"), K("c", 6)], w=[K("pytr")])
                cop("vector", lambda e, s_=s_: e.tensor_copy(
                    out=y_all[:, s_ * NH:(s_ + 1) * NH], in_=pytr[:]),
                    r=[K("pytr")], w=[K("yall", s_)])
            wfq = [sb(f"wfq{i}", (128, DFF // 2), c=c4) for i in range(2)]
            wbq = [sb(f"wbq{i}", (128, DFF // 2), BF16, c=c4) for i in range(2)]
            sWF = [sem("sWF0"), sem("sWF1")]
            sWQ = [sem("sWQ0"), sem("sWQ1")]
            NCHQ = W1CH + 2 * W2CH

            def p1_quant_chunk(c):
                b = c % 2
                if c < W1CH:
                    src = w1_in[c * 128:(c + 1) * 128, :]
                    dst = t1q[c * 128:(c + 1) * 128, :]
                    width = DM
                    wcol = 0
                else:
                    cc = c - W1CH
                    rows = slice((cc // 2) * 128, (cc // 2 + 1) * 128)
                    cols = slice((cc % 2) * (DFF // 2),
                                 (cc % 2 + 1) * (DFF // 2))
                    src = w2_in[rows, cols]
                    dst = t2q[rows, cols]
                    width = DFF // 2
                    wcol = 1
                p.dma("sync", sWF[b], wfq[b][:, 0:width], src, w=[K("wfq", b)])
                cop("vector", lambda e, b=b, width=width, wcol=wcol:
                    e.tensor_scalar(
                        out=wfq[b][:, 0:width], in0=wfq[b][:, 0:width],
                        scalar1=c_ws12[:, wcol:wcol + 1], scalar2=MAGIC,
                        op0=ALU.mult, op1=ALU.add),
                    r=[K("wfq", b), K("cws12")], w=[K("wfq", b)])
                cop("scalar", lambda e, b=b, width=width: e.activation(
                    out=wfq[b][:, 0:width], in_=wfq[b][:, 0:width],
                    func=AF.Identity, bias=c_negmag[:]),
                    r=[K("wfq", b), K("c", "nm")], w=[K("wfq", b)])
                cop("vector", lambda e, b=b, width=width: e.tensor_scalar(
                    out=wbq[b][:, 0:width], in0=wfq[b][:, 0:width],
                    scalar1=-1.0, scalar2=1.0, op0=ALU.max, op1=ALU.min),
                    r=[K("wfq", b)], w=[K("wbq", b)])
                p.dma("scalar", sWQ[b], dst, wbq[b][:, 0:width],
                      r=[K("wbq", b)], w=[K("tq", c)])

            for c in range(NCHQ):
                p1_quant_chunk(c)

            tt = [sb(f"tt{i}", (128, DFF), BF16, c=c4) for i in range(2)]
            sTT = [sem("sTT0"), sem("sTT1")]
            sTB = [sem("sTB0"), sem("sTB1")]
            for d in range(NK):
                b = d % 2
                p.dmat("sync", sTT[b], tt[b][:, 0:DFF],
                       t1q[0:DFF, d * 128:(d + 1) * 128],
                       r=[K("tq", c) for c in range(W1CH)], w=[K("tt", b)])
                for n in range(NSL1):
                    p.dma("gpsimd", sTB[b], t1b[n, :, d, :],
                          tt[b][:, n * 512:(n + 1) * 512],
                          r=[K("tt", b)], w=[K("t1b", n, d)])
            p.barrier()
        for s_ in range(NSUP):
            nsub_s = TSUP // SUB
            with ExitStack() as c6:
                if s_ == 0:
                    tt2 = [sb(f"tt2_{i}", (128, DM), BF16, c=c6)
                           for i in range(2)]
                    sT2T = [sem("sT2T0"), sem("sT2T1")]
                    sT2B = [sem("sT2B0"), sem("sT2B1")]
                    for f in range(NKF):
                        b2 = f % 2
                        p.dmat("sync", sT2T[b2], tt2[b2][:, 0:DM],
                               t2q[0:DM, f * 128:(f + 1) * 128],
                               w=[K("tt2", b2)])
                        for n in range(NSL2):
                            p.dma("gpsimd", sT2B[b2], t2b[n, :, f, :],
                                  tt2[b2][:, n * 512:(n + 1) * 512],
                                  r=[K("tt2", b2)], w=[K("t2b", n, f)])
                q1T = sb(f"q1T{s_}", (128, NK * TSUP), BF16, c=c6)
                rhs1 = [sb(f"rhs1_{s_}_{i}", (128, NK * 512), BF16, c=c6)
                        for i in range(2)]
                t_h = [sb(f"t_h{s_}_{i}", (128, 512), c=c6) for i in range(4)]
                cpam = sb(f"cpam{s_}", (128, 1), c=c6)
                pw1 = [ps(f"pw1_{s_}_{i}", (128, 512), c=c6) for i in range(4)]
                sQT = sem(f"sQT{s_}")
                sR1 = [sem(f"sR1a{s_}"), sem(f"sR1b{s_}")]
                sH = [sem(f"sH{s_}_{i}") for i in range(4)]
                for k in range(NK):
                    p.dmat("sync", sQT, q1T[:, k * TSUP:(k + 1) * TSUP],
                           q1buf[s_ * TSUP:(s_ + 1) * TSUP,
                                 k * 128:(k + 1) * 128],
                           r=[K("q1buf", s_ * nsub_s + j)
                              for j in range(nsub_s)],
                           w=[K("q1T", k)])
                for n in range(NSL1):
                    rb = n % 2
                    p.dma("gpsimd", sR1[rb], rhs1[rb][:],
                          t1b[n].rearrange("p k f -> p (k f)"),
                          r=[K("t1b", n, d) for d in range(NK)],
                          w=[K("rhs1", rb)])
                    for j in range(nsub_s):
                        pb = j % 4
                        for k in range(NK):
                            fn = (lambda e, k=k, j=j, pb=pb, rb=rb,
                                  pw1=pw1, q1T=q1T, rhs1=rhs1:
                                  e.matmul(
                                      out=pw1[pb][:],
                                      lhsT=q1T[:, k * TSUP + j * 128:
                                               k * TSUP + (j + 1) * 128],
                                      rhs=rhs1[rb][:, k * 512:(k + 1) * 512],
                                      start=(k == 0), stop=(k == NK - 1)))
                            if k < NK - 1:
                                copq("tensor", fn,
                                     r=[K("q1T", k), K("rhs1", rb)],
                                     w=([K("pw1", pb)] if k == 0 else []))
                            else:
                                cop("tensor", fn,
                                    r=[K("q1T", kk) for kk in range(NK)] +
                                      [K("rhs1", rb)],
                                    w=[K("pw1", pb)])
                        cop("scalar", lambda e, pb=pb, j=j, s_=s_,
                            t_h=t_h, pw1=pw1: e.activation(
                            out=t_h[pb][:], in_=pw1[pb][:], func=AF.Silu,
                            scale=evs1_all[:, s_ * nsub_s + j:
                                           s_ * nsub_s + j + 1]),
                            r=[K("pw1", pb), K("evs1")], w=[K("t_h", pb)])
                        cop("vector", lambda e, pb=pb, cpam=cpam, t_h=t_h:
                            e.reduce_max(
                            out=cpam[:], in_=t_h[pb][:], axis=AX.X,
                            apply_absolute_value=True),
                            r=[K("t_h", pb)], w=[K("cpam")])
                        if n == 0:
                            cop("vector", lambda e, j=j, cpam=cpam: e.tensor_copy(
                                out=amax2_all[:, j:j + 1], in_=cpam[:]),
                                r=[K("cpam")], w=[K("amax2", j)])
                        else:
                            cop("vector", lambda e, j=j, cpam=cpam: e.tensor_tensor(
                                out=amax2_all[:, j:j + 1],
                                in0=amax2_all[:, j:j + 1], in1=cpam[:],
                                op=ALU.max),
                                r=[K("cpam"), K("amax2", j)],
                                w=[K("amax2", j)])
                        p.dma("scalar", sH[pb],
                              hbuf[j * 128:(j + 1) * 128,
                                   n * 512:(n + 1) * 512],
                              t_h[pb][:], r=[K("t_h", pb)],
                              w=[K("hbuf", j, n)])
                p.barrier()
            with ExitStack() as c7:
                GT = 256
                NGRP = TSUP // GT
                NQ = NKF // 8
                q2Tg = [sb(f"q2Tg{s_}_{i}", (128, NKF * GT), BF16, c=c7)
                        for i in range(2)]
                rhs2 = [sb(f"rhs2_{s_}_{i}", (128, NQ * 512), BF16, c=c7)
                        for i in range(2)]
                t_hr = sb(f"t_hr{s_}", (128, DFF // 2), c=c7)
                t_q2b = [sb(f"t_q2b{s_}_{i}", (128, DFF), BF16, c=c7)
                         for i in range(4)]
                t_g = [sb(f"t_g{s_}_{i}", (128, 512), c=c7) for i in range(2)]
                cq = {nm: sb(f"cq{s_}_" + nm, (128, 1), c=c7) for nm in
                      ("am2c", "rec", "s2", "mag")}
                pq2 = [ps(f"pq2_{s_}_{i}", (128, 512), BF16, c=c7)
                       for i in range(2)]
                pw2 = [ps(f"pw2_{s_}_{i}", (128, 512), c=c7) for i in range(2)]
                sHR = sem(f"sHR{s_}")
                sR2 = [sem(f"sR2a{s_}"), sem(f"sR2b{s_}")]
                sG = [sem(f"sGa{s_}"), sem(f"sGb{s_}")]
                HH = DFF // 2
                cop("vector", lambda e, cq=cq: e.memset(cq["mag"][:], MAGIC),
                    w=[K("cq", "mag")])

                def p7_quant(j):
                    jb = j % 4
                    cop("vector", lambda e, j=j, cq=cq: e.tensor_scalar(
                        out=cq["am2c"][:], in0=amax2_all[:, j:j + 1],
                        scalar1=1e-5, scalar2=None, op0=ALU.max),
                        r=[K("amax2", j)], w=[K("cq", "am2c")])
                    cop("vector", lambda e, cq=cq: e.reciprocal(
                        out=cq["rec"][:], in_=cq["am2c"][:]),
                        r=[K("cq", "am2c")], w=[K("cq", "rec")])
                    cop("vector", lambda e, cq=cq: e.tensor_scalar(
                        out=cq["s2"][:], in0=cq["rec"][:], scalar1=127.0,
                        scalar2=None, op0=ALU.mult),
                        r=[K("cq", "rec")], w=[K("cq", "s2")])
                    cop("vector", lambda e, j=j, cq=cq: e.tensor_scalar(
                        out=evs2_all[:, j:j + 1], in0=cq["am2c"][:],
                        scalar1=c_mean12[:, 1:2], scalar2=1.0 / 127.0,
                        op0=ALU.mult, op1=ALU.mult),
                        r=[K("cq", "am2c"), K("cm12")], w=[K("evs2", j)])
                    for hh in range(2):
                        p.dma("scalar", sHR, t_hr[:],
                              hbuf[j * 128:(j + 1) * 128,
                                   hh * HH:(hh + 1) * HH],
                              r=[K("hbuf", j, n) for n in range(NSL1)],
                              w=[K("t_hr")])
                        cop("scalar", lambda e, t_hr=t_hr, cq=cq:
                            e.activation(
                                out=t_hr[:], in_=t_hr[:],
                                func=AF.Identity, scale=cq["s2"][:],
                                bias=cq["mag"][:]),
                            r=[K("t_hr"), K("cq", "s2"), K("cq", "mag")],
                            w=[K("t_hr")])
                        cop("vector", lambda e, t_hr=t_hr: e.tensor_scalar(
                            out=t_hr[:], in0=t_hr[:],
                            scalar1=MAGIC, scalar2=127.0,
                            op0=ALU.subtract, op1=ALU.min),
                            r=[K("t_hr")], w=[K("t_hr")])
                        cop("vector", lambda e, hh=hh, jb=jb,
                            t_q2b=t_q2b, t_hr=t_hr: e.tensor_scalar(
                                out=t_q2b[jb][:, hh * HH:(hh + 1) * HH],
                                in0=t_hr[:],
                                scalar1=-128.0, scalar2=None, op0=ALU.max),
                            r=[K("t_hr")], w=[K("t_q2b", jb, hh)])

                def p7_grp_tr(grp):
                    gb = grp % 2
                    for s2_ in range(2):
                        j = grp * 2 + s2_
                        jb = j % 4
                        for kg in range(NKF // 4):
                            pb = kg % 2
                            for kk in range(4):
                                k = kg * 4 + kk
                                fn = (lambda e, k=k, kk=kk, pb=pb, jb=jb,
                                      pq2=pq2, t_q2b=t_q2b: e.transpose(
                                          out=pq2[pb][:, kk * 128:
                                                      (kk + 1) * 128],
                                          in_=t_q2b[jb][:, k * 128:
                                                        (k + 1) * 128],
                                          identity=c_identb[:]))
                                if kk == 0:
                                    copq("tensor", fn,
                                         r=[K("t_q2b", jb, k // (NKF // 2)),
                                            K("c", 5)],
                                         w=[K("pq2", pb)])
                                elif kk < 3:
                                    copq("tensor", fn,
                                         r=[K("t_q2b", jb,
                                              k // (NKF // 2)), K("c", 5)])
                                else:
                                    cop("tensor", fn,
                                        r=[K("t_q2b", jb, 0),
                                           K("t_q2b", jb, 1), K("c", 5)],
                                        w=[K("pq2", pb)])
                            cop("vector", lambda e, kg=kg, s2_=s2_, gb=gb,
                                pb=pb, q2Tg=q2Tg, pq2=pq2:
                                e.tensor_copy(
                                    out=q2Tg[gb][:].rearrange(
                                        "p (k t) -> p k t", t=GT)[
                                        :, 4 * kg:4 * kg + 4,
                                        s2_ * 128:(s2_ + 1) * 128],
                                    in_=pq2[pb][:].rearrange(
                                        "p (k t) -> p k t", t=128)),
                                r=[K("pq2", pb)], w=[K("q2Tg", gb, kg)])

                def p7_grp_mms(grp):
                    gb = grp % 2
                    for n in range(NSL2):
                        for q in range(8):
                            rb = q % 2
                            p.dma("gpsimd", sR2[rb], rhs2[rb][:],
                                  t2b[n, :, q * NQ:(q + 1) * NQ, :]
                                  .rearrange("p k f -> p (k f)"),
                                  r=[K("t2b", n, f) for f in
                                     range(q * NQ, (q + 1) * NQ)],
                                  w=[K("rhs2", rb)])
                            for s2_ in range(2):
                                for kk in range(NQ):
                                    k = q * NQ + kk
                                    fn = (lambda e, k=k, kk=kk, rb=rb, s2_=s2_,
                                          gb=gb, pw2=pw2, q2Tg=q2Tg, rhs2=rhs2:
                                          e.matmul(
                                              out=pw2[s2_][:],
                                              lhsT=q2Tg[gb][:,
                                                           k * GT + s2_ * 128:
                                                           k * GT +
                                                           (s2_ + 1) * 128],
                                              rhs=rhs2[rb][:, kk * 512:
                                                           (kk + 1) * 512],
                                              start=(k == 0),
                                              stop=(k == NKF - 1)))
                                    if k == NKF - 1:
                                        cop("tensor", fn,
                                            r=[K("q2Tg", gb, kk2)
                                               for kk2 in range(NKF // 4)] +
                                              [K("rhs2", 0), K("rhs2", 1)],
                                            w=[K("pw2", s2_)])
                                    elif k == 0:
                                        cop("tensor", fn,
                                            r=[K("q2Tg", gb, 0),
                                               K("rhs2", rb)],
                                            w=[K("pw2", s2_)])
                                    elif kk == NQ - 1:
                                        cop("tensor", fn,
                                            r=[K("q2Tg", gb, k // 4),
                                               K("rhs2", rb)], w=[])
                                    else:
                                        copq("tensor", fn,
                                             r=[K("q2Tg", gb, k // 4),
                                                K("rhs2", rb)])
                        for s2_ in range(2):
                            j = grp * 2 + s2_
                            cop("scalar", lambda e, s2_=s2_, j=j,
                                t_g=t_g, pw2=pw2: e.activation(
                                out=t_g[s2_][:], in_=pw2[s2_][:],
                                func=AF.Silu,
                                scale=evs2_all[:, j:j + 1]),
                                r=[K("pw2", s2_), K("evs2", j)],
                                w=[K("t_g", s2_)])
                            p.dma("scalar", sG[s2_],
                                  gbuf[(s_ * TSUP + j * 128):
                                       (s_ * TSUP + (j + 1) * 128),
                                       n * 512:(n + 1) * 512],
                                  t_g[s2_][:], r=[K("t_g", s2_)],
                                  w=[K("gbuf", s_ * (TSUP // SUB) + j, n)])

                for j in range(min(4, TSUP // SUB)):
                    p7_quant(j)
                p7_grp_tr(0)
                for grp in range(NGRP):
                    if grp + 1 < NGRP:
                        p7_grp_tr(grp + 1)
                    for j in range(2 * (grp + 2), 2 * (grp + 3)):
                        if j < TSUP // SUB:
                            p7_quant(j)
                    p7_grp_mms(grp)
                p.barrier()

        with ExitStack() as cb1:
            c_dp = sb("c_dp", (128, DM), c=cb1)
            p.dma("sync", sC, c_dp[:], dp_in, w=[K("c", 9)])
            wgset = sb("wgset", (128, NK * DM), BF16, c=cb1)
            xnT = sb("xnT", (128, NK * 512), BF16, c=cb1)
            t_xb = [sb(f"t_xb{i}", (128, DM), c=cb1) for i in range(2)]
            t_sqb = sb("t_sqb", (128, DM), c=cb1)
            t_xn1 = [sb(f"t_xn1{i}", (128, DM), c=cb1) for i in range(2)]
            t_gate = [sb(f"t_gate{i}", (128, DM), c=cb1) for i in range(2)]
            t_yh = [sb(f"t_yh{i}", (128, DM), c=cb1) for i in range(2)]
            t_yfb = [sb(f"t_yfb{i}", (128, DM), BF16, c=cb1) for i in range(2)]
            clb = [{nm: sb(f"clb{i}_" + nm, (128, 1), c=cb1) for nm in
                    ("sum", "sq", "m", "m2", "v", "rstd", "negmr")}
                   for i in range(2)]
            pg = [ps(f"pg{i}", (128, 512), c=cb1) for i in range(8)]
            sWG = sem("sWG")
            sXB = [sem("sXB0"), sem("sXB1")]
            sXT = sem("sXT")
            sYF = [sem("sYF0"), sem("sYF1")]
            for k in range(NK):
                p.dma("gpsimd", sWG, wgset[:, k * DM:(k + 1) * DM],
                      wg_in[k * 128:(k + 1) * 128, :], w=[K("wgset", k)])

            def ln_b(t_xt, t_sqt, clx, keyx, t_out, keyout, kb):
                cop("vector", lambda e, t_xt=t_xt, clx=clx: e.reduce_sum(
                    out=clx["sum"][:], in_=t_xt[:], axis=AX.X),
                    r=[keyx], w=[K("clb", kb, "sum")])
                cop("scalar", lambda e, t_xt=t_xt, t_sqt=t_sqt, clx=clx:
                    e.activation(
                    out=t_sqt[:], in_=t_xt[:], func=AF.Square,
                    accum_out=clx["sq"][:]),
                    r=[keyx], w=[K("tsqb"), K("clb", kb, "sq")])
                cop("vector", lambda e, clx=clx: e.tensor_scalar(
                    out=clx["m"][:], in0=clx["sum"][:], scalar1=1.0 / DM,
                    scalar2=None, op0=ALU.mult),
                    r=[K("clb", kb, "sum")], w=[K("clb", kb, "m")])
                cop("vector", lambda e, clx=clx: e.tensor_tensor(
                    out=clx["m2"][:], in0=clx["m"][:], in1=clx["m"][:],
                    op=ALU.mult), r=[K("clb", kb, "m")], w=[K("clb", kb, "m2")])
                cop("vector", lambda e, clx=clx: e.tensor_scalar(
                    out=clx["v"][:], in0=clx["sq"][:], scalar1=1.0 / DM,
                    scalar2=None, op0=ALU.mult),
                    r=[K("clb", kb, "sq")], w=[K("clb", kb, "v")])
                cop("vector", lambda e, clx=clx: e.tensor_tensor(
                    out=clx["v"][:], in0=clx["v"][:], in1=clx["m2"][:],
                    op=ALU.subtract), r=[K("clb", kb, "m2"), K("clb", kb, "v")],
                    w=[K("clb", kb, "v")])
                cop("scalar", lambda e, clx=clx: e.activation(
                    out=clx["rstd"][:], in_=clx["v"][:], func=AF.Ln,
                    bias=c_eps[:]),
                    r=[K("clb", kb, "v"), K("c", "eps")],
                    w=[K("clb", kb, "rstd")])
                cop("scalar", lambda e, clx=clx: e.activation(
                    out=clx["rstd"][:], in_=clx["rstd"][:], func=AF.Exp,
                    scale=-0.5), r=[K("clb", kb, "rstd")],
                    w=[K("clb", kb, "rstd")])
                cop("vector", lambda e, clx=clx: e.tensor_scalar(
                    out=clx["negmr"][:], in0=clx["m"][:],
                    scalar1=clx["rstd"][:], scalar2=-1.0, op0=ALU.mult,
                    op1=ALU.mult),
                    r=[K("clb", kb, "m"), K("clb", kb, "rstd")],
                    w=[K("clb", kb, "negmr")])
                cop("scalar", lambda e, t_xt=t_xt, t_out=t_out, clx=clx:
                    e.activation(
                    out=t_out[:], in_=t_xt[:], func=AF.Identity,
                    scale=clx["rstd"][:], bias=clx["negmr"][:]),
                    r=[keyx, K("clb", kb, "rstd"), K("clb", kb, "negmr")],
                    w=[keyout])

            for sub in range(NSUB):
                buf = sub % 2
                if sub % 4 == 0:
                    grp = sub // 4
                    for k in range(NK):
                        p.dmat("sync", sXT, xnT[:, k * 512:(k + 1) * 512],
                               xnbuf[NWARM + grp * 512:NWARM + (grp + 1) * 512,
                                     k * 128:(k + 1) * 128],
                               r=[K("xnbuf", (NWARM // SUB) + grp * 4 + j)
                                  for j in range(4)],
                               w=[K("xnT", k)])
                p.dma("gpsimd", sXB[buf], t_xb[buf][:],
                      x_in[(NWARM + sub * 128):(NWARM + (sub + 1) * 128), :],
                      w=[K("t_xb", buf)])
                s4 = sub % 4
                for n in range(NSL2):
                    pb = (sub * 4 + n) % 8
                    for k in range(NK):
                        copq("tensor", lambda e, k=k, s4=s4, n=n, pb=pb:
                             e.matmul(
                                 out=pg[pb][:],
                                 lhsT=xnT[:, k * 512 + s4 * 128:
                                          k * 512 + (s4 + 1) * 128],
                                 rhs=wgset[:, k * DM + n * 512:
                                           k * DM + (n + 1) * 512],
                                 start=(k == 0), stop=False),
                             r=[K("xnT", k), K("wgset", k)],
                             w=([K("pg", pb)] if k == 0 else []))
                    cop("tensor", lambda e, n=n, pb=pb: e.matmul(
                        out=pg[pb][:], lhsT=c_ones_row[:],
                        rhs=c_bgb[0:1, n * 512:(n + 1) * 512],
                        start=False, stop=True),
                        r=[K("c", 1), K("c", 10)] +
                          [K("xnT", kk) for kk in range(NK)] +
                          [K("wgset", kk) for kk in range(NK)],
                        w=[K("pg", pb)])
                    cop("scalar", lambda e, n=n, pb=pb, buf=buf:
                        e.activation(
                        out=t_gate[buf][:, n * 512:(n + 1) * 512],
                        in_=pg[pb][:], func=AF.Sigmoid),
                        r=[K("pg", pb)], w=[K("gate", buf, n)])
                ln_b(t_xb[buf], t_sqb, clb[buf], K("t_xb", buf),
                     t_xn1[buf], K("t_xn1", buf), buf)
                cop("vector", lambda e, buf=buf: e.tensor_tensor(
                    out=t_yh[buf][:], in0=t_xn1[buf][:], in1=c_dp[:],
                    op=ALU.mult),
                    r=[K("t_xn1", buf), K("c", 9)], w=[K("t_yh", buf)])

                def yh_add(e, sub=sub, buf=buf):
                    y3 = y_all[:, sub * NH:(sub + 1) * NH].rearrange(
                        "p (h o) -> p h o", o=1)
                    t3 = t_yh[buf][:].rearrange("p (h d) -> p h d", d=64)
                    y3b, t3b = bass.broadcast_tensor_aps(y3, t3)
                    return e.tensor_tensor(out=t3, in0=t3b, in1=y3b, op=ALU.add)
                cop("vector", yh_add,
                    r=[K("t_yh", buf), K("yall", sub)], w=[K("t_yh", buf)])
                cop("vector", lambda e, buf=buf: e.tensor_tensor(
                    out=t_yfb[buf][:], in0=t_yh[buf][:], in1=t_gate[buf][:],
                    op=ALU.mult),
                    r=[K("t_yh", buf)] + [K("gate", buf, n)
                                          for n in range(NSL2)],
                    w=[K("t_yfb", buf)])
                p.dma("scalar", sYF[buf], yfbuf[sub * 128:(sub + 1) * 128, :],
                      t_yfb[buf][:], r=[K("t_yfb", buf)], w=[K("yfbuf", sub)])
            p.barrier()

        with ExitStack() as cb2:
            woset = sb("woset", (128, NK * DM), BF16, c=cb2)
            yfT = sb("yfT", (128, NK * 512), BF16, c=cb2)
            t_s = [sb(f"t_s{i}", (128, DM), c=cb2) for i in range(2)]
            t_gr = [sb(f"t_gr{i}", (128, DM), c=cb2) for i in range(2)]
            t_xb2 = [sb(f"t_xb2{i}", (128, DM), c=cb2) for i in range(2)]
            t_o = [sb(f"t_o{i}", (128, DM), c=cb2) for i in range(2)]
            po = [ps(f"po{i}", (128, 512), c=cb2) for i in range(8)]
            sWO = sem("sWO")
            sYT = sem("sYT")
            sGR = [sem("sGR0"), sem("sGR1")]
            sXB2 = [sem("sXB20"), sem("sXB21")]
            sOUT = [sem("sOUT0"), sem("sOUT1")]
            for k in range(NK):
                p.dma("gpsimd", sWO, woset[:, k * DM:(k + 1) * DM],
                      wo_in[k * 128:(k + 1) * 128, :], w=[K("woset", k)])
            for sub in range(NSUB):
                buf = sub % 2
                if sub % 4 == 0:
                    grp = sub // 4
                    for k in range(NK):
                        p.dmat("sync", sYT, yfT[:, k * 512:(k + 1) * 512],
                               yfbuf[grp * 512:(grp + 1) * 512,
                                     k * 128:(k + 1) * 128],
                               r=[K("yfbuf", grp * 4 + j) for j in range(4)],
                               w=[K("yfT", k)])
                p.dma("gpsimd", sGR[buf], t_gr[buf][:],
                      gbuf[sub * 128:(sub + 1) * 128, :],
                      r=[K("gbuf", sub, n) for n in range(NSL2)],
                      w=[K("t_gr", buf)])
                p.dma("gpsimd", sXB2[buf], t_xb2[buf][:],
                      x_in[(NWARM + sub * 128):(NWARM + (sub + 1) * 128), :],
                      w=[K("t_xb2", buf)])
                s4 = sub % 4
                for n in range(NSL2):
                    pb = (sub * 4 + n) % 8
                    for k in range(NK):
                        fn = (lambda e, k=k, s4=s4, n=n, pb=pb:
                              e.matmul(
                                  out=po[pb][:],
                                  lhsT=yfT[:, k * 512 + s4 * 128:
                                           k * 512 + (s4 + 1) * 128],
                                  rhs=woset[:, k * DM + n * 512:
                                            k * DM + (n + 1) * 512],
                                  start=(k == 0), stop=(k == NK - 1)))
                        if k < NK - 1:
                            copq("tensor", fn,
                                 r=[K("yfT", k), K("woset", k)],
                                 w=([K("po", pb)] if k == 0 else []))
                        else:
                            cop("tensor", fn,
                                r=[K("yfT", kk) for kk in range(NK)] +
                                  [K("woset", kk) for kk in range(NK)],
                                w=[K("po", pb)])
                    cop("scalar", lambda e, n=n, pb=pb, buf=buf:
                        e.activation(
                        out=t_s[buf][:, n * 512:(n + 1) * 512], in_=po[pb][:],
                        func=AF.Identity),
                        r=[K("po", pb)], w=[K("t_s", buf, n)])
                cop("vector", lambda e, buf=buf: e.tensor_tensor(
                    out=t_o[buf][:], in0=t_s[buf][:], in1=t_gr[buf][:],
                    op=ALU.mult),
                    r=[K("t_gr", buf)] + [K("t_s", buf, n)
                                          for n in range(NSL2)],
                    w=[K("t_o", buf)])
                cop("vector", lambda e, buf=buf: e.tensor_tensor(
                    out=t_o[buf][:], in0=t_o[buf][:], in1=t_xb2[buf][:],
                    op=ALU.add),
                    r=[K("t_o", buf), K("t_xb2", buf)], w=[K("t_o", buf)])
                p.dma("scalar", sOUT[buf], out_d[sub * 128:(sub + 1) * 128, :],
                      t_o[buf][:], r=[K("t_o", buf)], w=[K("outd", sub)])
            p.fence("gpsimd", [K("outd", sub) for sub in range(NSUB)])

    p.replay()
    return nc


def _shard_inputs(L0_attn_x, L0_attn_Wx, L0_attn_Wg, L0_attn_bg, L0_attn_Wo,
                  L0_attn_A, L0_attn_Dp, L0_ffn_W1, L0_ffn_W2):
    x = np.asarray(L0_attn_x, np.float32)
    wx = np.ascontiguousarray(np.asarray(L0_attn_Wx, np.float32))
    wg = np.ascontiguousarray(np.asarray(L0_attn_Wg, np.float32))
    wo = np.ascontiguousarray(np.asarray(L0_attn_Wo, np.float32))
    w1 = np.ascontiguousarray(np.asarray(L0_ffn_W1, np.float32))
    w2 = np.ascontiguousarray(np.asarray(L0_ffn_W2, np.float32))
    bg = np.ascontiguousarray(np.asarray(L0_attn_bg, np.float32).reshape(1, DM))
    at = np.ascontiguousarray(np.asarray(L0_attn_A, np.float32).T)
    dp = np.asarray(L0_attn_Dp, np.float32)
    dp_rep = np.ascontiguousarray(
        np.broadcast_to(np.repeat(dp, DM // NH)[None, :], (128, DM)))
    ones_row = np.ones((1, 128), ml_dtypes.bfloat16)
    ones_col = np.ones((128, 1), ml_dtypes.bfloat16)
    ones_row_f = np.ones((1, 128), np.float32)
    ones_col_f = np.ones((128, 1), np.float32)
    ident_b = np.eye(128, dtype=ml_dtypes.bfloat16)
    ident32_f = np.eye(32, dtype=np.float32)
    sel_h = np.zeros((NH, NH * 128), ml_dtypes.bfloat16)
    for h in range(NH):
        sel_h[h, h * 128:(h + 1) * 128] = 1.0

    in_maps = []
    spans = x.shape[1] // 4
    for k in range(8):
        bi, si = k // 4, k % 4
        lo = si * spans
        if si == 0:
            xpad = np.concatenate(
                [np.zeros((NWARM, DM), np.float32), x[bi, lo:lo + spans]],
                axis=0)
            woff = np.full((128, 1), -1e9, np.float32)
        else:
            xpad = np.ascontiguousarray(x[bi, lo - NWARM:lo + spans])
            woff = np.zeros((128, 1), np.float32)
        in_maps.append(dict(
            x=xpad, wx=wx, wg=wg, wo=wo, w1=w1, w2=w2, bg=bg, dp_rep=dp_rep,
            a_t=at, ones_row=ones_row, ones_col=ones_col,
            ones_row_f=ones_row_f, ones_col_f=ones_col_f, ident_b=ident_b,
            ident32_f=ident32_f, woff=woff, sel_h=sel_h))
    return in_maps


def kernel(**inputs):
    in_maps = _shard_inputs(**inputs)
    nc = bass.Bass("TRN2", target_bir_lowering=False)
    build_program(nc)
    res = run_bass_kernel_spmd(nc, in_maps, core_ids=list(range(8)))
    shards = [np.asarray(r["out"], np.float32) for r in res.results]
    out = np.empty((2, 16384, DM), np.float32)
    for k in range(8):
        bi, si = k // 4, k % 4
        out[bi, si * 4096:(si + 1) * 4096] = shards[k]
    return out


# version 22
# speedup vs baseline: 1.0494x; 1.0494x over previous
import numpy as np
import ml_dtypes
import concourse.bass as bass
import concourse.mybir as mybir
from concourse.bass_utils import run_bass_kernel_spmd
from contextlib import ExitStack

F32 = mybir.dt.float32
BF16 = mybir.dt.bfloat16
I8 = mybir.dt.int8
AF = mybir.ActivationFunctionType
ALU = mybir.AluOpType
AX = mybir.AxisListType

MAGIC = 12582912.0
NT = 4096
NWARM = 512
NTW = NT + NWARM
DM = 2048
DFF = 8192
NS = 128
NH = DM // 64
NSSM = 2 * NS + NH
SUB = 128
NSUB = NT // SUB
NSUBW = NTW // SUB
NSUP = 2
TSUP = NT // NSUP
LN_EPS = 1e-5

ENGINES = ("sync", "scalar", "vector", "tensor", "gpsimd")


class Sem:
    __slots__ = ("h", "n", "dma")

    def __init__(self, handle, dma=False):
        self.h = handle
        self.n = 0
        self.dma = dma

    def inc(self, k):
        self.n += k
        return self.n


class Prog:

    def __init__(self, nc):
        self.nc = nc
        self.st = {k: [] for k in ENGINES}
        self.res = {}
        self.water = {k: {} for k in ENGINES}
        self.sems = []

    def em(self, eng, fn):
        self.st[eng].append(fn)

    def raw_wait(self, eng, sem, n):
        if sem.dma:
            n = sem.n
        if n <= 0:
            return
        w = self.water[eng]
        if w.get(sem, 0) >= n:
            return
        w[sem] = n
        self.em(eng, lambda e, s=sem.h, n=n: e.wait_ge(s, n))

    def _need(self, eng, reads, writes):
        waits = []
        for k in reads:
            r = self.res.get(k)
            if r and r["w"]:
                waits.append(r["w"])
        for k in writes:
            r = self.res.get(k)
            if r:
                if r["w"]:
                    waits.append(r["w"])
                waits.extend(r["r"])
        for sem, n in waits:
            self.raw_wait(eng, sem, n)

    def _record(self, mark, reads, writes):
        for k in writes:
            self.res[k] = {"w": mark, "r": []}
        for k in reads:
            r = self.res.setdefault(k, {"w": None, "r": []})
            r["r"].append(mark)

    def dma(self, eng, sem, out, in_, r=(), w=(), **kw):
        sem.dma = True
        self._need(eng, r, w)
        n = sem.inc(16)
        self.em(eng, lambda e, o=out, i=in_, s=sem.h, kw=kw:
                e.dma_start(out=o, in_=i, **kw).then_inc(s, 16))
        self._record((sem, n), r, w)
        return n

    def dmat(self, eng, sem, out, in_, r=(), w=()):
        sem.dma = True
        self._need(eng, r, w)
        n = sem.inc(16)
        self.em(eng, lambda e, o=out, i=in_, s=sem.h:
                e.dma_start_transpose(out=o, in_=i).then_inc(s, 16))
        self._record((sem, n), r, w)
        return n

    def op(self, eng, sem, fn, r=(), w=()):
        self._need(eng, r, w)
        n = sem.inc(1)
        self.em(eng, lambda e, fn=fn, s=sem.h: fn(e).then_inc(s, 1))
        self._record((sem, n), r, w)
        return n

    def op_nomark(self, eng, fn, r=(), w=()):
        self._need(eng, r, w)
        self.em(eng, fn)

    def fence(self, eng, keys):
        waits = []
        for k in keys:
            r = self.res.get(k)
            if r:
                if r["w"]:
                    waits.append(r["w"])
                waits.extend(r["r"])
        for sem, n in waits:
            self.raw_wait(eng, sem, n)

    def barrier(self):
        for eng in ENGINES:
            for s in self.sems:
                self.raw_wait(eng, s, s.n)
        self.res.clear()

    def replay(self):
        with self.nc.Block() as block:
            for eng in ENGINES:
                deco = getattr(block, eng)
                stream = self.st[eng]

                def make(stream):
                    def body(e):
                        for fn in stream:
                            fn(e)
                    return body
                deco(make(stream))


def K(*a):
    return tuple(a)


def build_program(nc):
    NK = DM // SUB
    NKF = DFF // SUB
    NSL1 = DFF // 512
    NSL2 = DM // 512
    W1CH = DFF // SUB
    W2CH = DM // SUB
    NCHUNK = NTW // 512
    P4SLAB = NTW // 4
    NELW = DFF * DM

    def din(name, shape, dt=F32):
        return nc.dram_tensor(name, list(shape), dt, kind="ExternalInput").ap()

    def dscr(name, shape, dt):
        return nc.dram_tensor(name, list(shape), dt, kind="Internal").ap()

    x_in = din("x", (NTW, DM))
    wx_in = din("wx", (DM, NSSM))
    wg_in = din("wg", (DM, DM))
    wo_in = din("wo", (DM, DM))
    w1_in = din("w1", (DFF, DM))
    w2_in = din("w2", (DM, DFF))
    bg_in = din("bg", (1, DM))
    dp_in = din("dp_rep", (128, DM))
    at_in = din("a_t", (NS, NH))
    onesrow_in = din("ones_row", (1, 128), BF16)
    onescol_in = din("ones_col", (128, 1), BF16)
    onesrowf_in = din("ones_row_f", (1, 128))
    onescolf_in = din("ones_col_f", (128, 1))
    identb_in = din("ident_b", (128, 128), BF16)
    ident32f_in = din("ident32_f", (32, 32))
    woff_in = din("woff", (128, 1))
    sel_in = din("sel_h", (NH, NH * 128), BF16)

    out_d = nc.dram_tensor("out", [NT, DM], F32, kind="ExternalOutput").ap()

    xnbuf = dscr("xnbuf", (NTW, DM), BF16)
    q1buf = dscr("q1buf", (NT, DM), BF16)
    ssmbuf = dscr("ssmbuf", (NTW, NSSM), BF16)
    hbuf = dscr("hbuf", (TSUP, DFF), F32)
    gbuf = dscr("gbuf", (NT, DM), F32)
    yfbuf = dscr("yfbuf", (NT, DM), BF16)
    t1q = dscr("t1q", (DFF, DM), BF16)
    t2q = dscr("t2q", (DM, DFF), BF16)
    ybuf = dscr("ybuf", (NH, NT), F32)
    q2buf = dscr("q2buf", (TSUP, DFF), BF16)
    t1b = dscr("t1b", (NSL1, 128, NK, 512), I8)
    t2b = dscr("t2b", (NSL2, 128, NKF, 512), I8)

    p = Prog(nc)
    with ExitStack() as ctx:
        def sb(name, shape, dt=F32, c=ctx):
            return c.enter_context(nc.sbuf_tensor(name, list(shape), dt))

        def ps(name, shape, dt=F32, c=ctx):
            return c.enter_context(nc.psum_tensor(name, list(shape), dt))

        def sem(name):
            s = Sem(ctx.enter_context(nc.semaphore(name)))
            p.sems.append(s)
            return s

        eV, eT, eA, eG = sem("eV"), sem("eT"), sem("eA"), sem("eG")
        ESEM = {"vector": eV, "tensor": eT, "scalar": eA, "gpsimd": eG}

        def cop(eng, fn, r=(), w=()):
            return p.op(eng, ESEM[eng], fn, r=r, w=w)

        def copq(eng, fn, r=(), w=()):
            p.op_nomark(eng, fn, r=r, w=w)

        c_ones_row = sb("c_ones_row", (1, 128), BF16)
        c_ones_col = sb("c_ones_col", (128, 1), BF16)
        c_ones_rowf = sb("c_ones_rowf", (1, 128))
        c_ones_colf = sb("c_ones_colf", (128, 1))
        c_identb = sb("c_identb", (128, 128), BF16)
        c_ident32f = sb("c_ident32f", (32, 32))
        c_at = sb("c_at", (NS, NH))
        c_woff = sb("c_woff", (128, 1))
        c_bgb = sb("c_bgb", (1, DM), BF16)
        c_eps = sb("c_eps", (128, 1))
        c_negmag = sb("c_negmag", (128, 1))
        c_sel = sb("c_sel", (NH, NH * 128), BF16)
        c_mean12 = sb("c_mean12", (128, 2))
        c_ws12 = sb("c_ws12", (128, 2))
        evs1_all = sb("evs1_all", (128, NSUB))
        amax1_all = sb("amax1_all", (128, NSUB))
        amax2_all = sb("amax2_all", (128, TSUP // SUB))
        evs2_all = sb("evs2_all", (128, TSUP // SUB))
        y_all = sb("y_all", (128, NSUB * NH))

        sC, sCg = sem("sC"), sem("sCg")
        p.dma("sync", sC, c_ones_row[:], onesrow_in, w=[K("c", 1)])
        p.dma("sync", sC, c_ones_col[:], onescol_in, w=[K("c", 2)])
        p.dma("sync", sC, c_ones_rowf[:], onesrowf_in, w=[K("c", 3)])
        p.dma("sync", sC, c_ones_colf[:], onescolf_in, w=[K("c", 4)])
        p.dma("sync", sC, c_identb[:], identb_in, w=[K("c", 5)])
        p.dma("sync", sC, c_ident32f[:], ident32f_in, w=[K("c", 6)])
        p.dma("sync", sC, c_at[:], at_in, w=[K("c", 7)])
        p.dma("sync", sC, c_woff[:], woff_in, w=[K("c", 8)])
        p.dma("sync", sC, c_sel[:], sel_in, w=[K("c", 11)])
        p.dma("gpsimd", sCg, c_bgb[:], bg_in, w=[K("c", 10)])
        cop("vector", lambda e: e.memset(c_eps[:], LN_EPS), w=[K("c", "eps")])
        cop("vector", lambda e: e.memset(c_negmag[:], -MAGIC), w=[K("c", "nm")])
        with ExitStack() as c3:
            t_x = [sb(f"t_x{i}", (128, DM), c=c3) for i in range(2)]
            t_sq = sb("t_sq", (128, DM), c=c3)
            t_xn = [sb(f"t_xn{i}", (128, DM), c=c3) for i in range(2)]
            t_xnb = [sb(f"t_xnb{i}", (128, DM), BF16, c=c3) for i in range(2)]
            t_q1m = sb("t_q1m", (128, DM), c=c3)
            t_q1b = [sb(f"t_q1b{i}", (128, DM), BF16, c=c3) for i in range(2)]
            cl = [{nm: sb(f"cl{i}_" + nm, (128, 1), c=c3) for nm in
                   ("sum", "sq", "m", "m2", "v", "rstd", "negmr",
                    "amax", "am1c", "rec", "s1")} for i in range(2)]
            sX = [sem("sX0"), sem("sX1")]
            sXN = [sem("sXN0"), sem("sXN1")]
            sQ1 = [sem("sQ10"), sem("sQ11")]
            wb = [sb(f"wb{i}", (128, DFF), BF16, c=c3) for i in range(2)]
            wf = [sb(f"wf{i}", (128, DFF), c=c3) for i in range(2)]
            wacc = sb("wacc", (128, W1CH + W2CH), c=c3)
            rs2 = sb("rs2", (128, 2), c=c3)
            m12s = sb("m12s", (1, 2), c=c3)
            pm11 = ps("pm11", (1, 2), c=c3)
            pm128 = ps("pm128", (128, 2), c=c3)
            sWB = [sem("sWB0"), sem("sWB1")]
            sWF = [sem("sWF0"), sem("sWF1")]
            sWQ = [sem("sWQ0"), sem("sWQ1")]

            def ln_subtile(b, keyx):
                clb_ = cl[b]
                cop("vector", lambda e, b=b, clb_=clb_: e.reduce_sum(
                    out=clb_["sum"][:], in_=t_x[b][:], axis=AX.X),
                    r=[keyx], w=[K("cl", b, "sum")])
                cop("scalar", lambda e, b=b, clb_=clb_: e.activation(
                    out=t_sq[:], in_=t_x[b][:], func=AF.Square,
                    accum_out=clb_["sq"][:]),
                    r=[keyx], w=[K("t_sq"), K("cl", b, "sq")])
                cop("vector", lambda e, clb_=clb_: e.tensor_scalar(
                    out=clb_["m"][:], in0=clb_["sum"][:], scalar1=1.0 / DM,
                    scalar2=None, op0=ALU.mult),
                    r=[K("cl", b, "sum")], w=[K("cl", b, "m")])
                cop("vector", lambda e, clb_=clb_: e.tensor_tensor(
                    out=clb_["m2"][:], in0=clb_["m"][:], in1=clb_["m"][:],
                    op=ALU.mult), r=[K("cl", b, "m")], w=[K("cl", b, "m2")])
                cop("vector", lambda e, clb_=clb_: e.tensor_scalar(
                    out=clb_["v"][:], in0=clb_["sq"][:], scalar1=1.0 / DM,
                    scalar2=None, op0=ALU.mult),
                    r=[K("cl", b, "sq")], w=[K("cl", b, "v")])
                cop("vector", lambda e, clb_=clb_: e.tensor_tensor(
                    out=clb_["v"][:], in0=clb_["v"][:], in1=clb_["m2"][:],
                    op=ALU.subtract), r=[K("cl", b, "m2"), K("cl", b, "v")],
                    w=[K("cl", b, "v")])
                cop("scalar", lambda e, clb_=clb_: e.activation(
                    out=clb_["rstd"][:], in_=clb_["v"][:], func=AF.Ln,
                    bias=c_eps[:]),
                    r=[K("cl", b, "v"), K("c", "eps")], w=[K("cl", b, "rstd")])
                cop("scalar", lambda e, clb_=clb_: e.activation(
                    out=clb_["rstd"][:], in_=clb_["rstd"][:], func=AF.Exp,
                    scale=-0.5), r=[K("cl", b, "rstd")], w=[K("cl", b, "rstd")])
                cop("vector", lambda e, clb_=clb_: e.tensor_scalar(
                    out=clb_["negmr"][:], in0=clb_["m"][:],
                    scalar1=clb_["rstd"][:],
                    scalar2=-1.0, op0=ALU.mult, op1=ALU.mult),
                    r=[K("cl", b, "m"), K("cl", b, "rstd")],
                    w=[K("cl", b, "negmr")])
                cop("scalar", lambda e, b=b, clb_=clb_: e.activation(
                    out=t_xn[b][:], in_=t_x[b][:], func=AF.Identity,
                    scale=clb_["rstd"][:], bias=clb_["negmr"][:]),
                    r=[keyx, K("cl", b, "rstd"), K("cl", b, "negmr")],
                    w=[K("t_xn", b)])

            def p3_subtile(i):
                buf = i % 2
                real = i >= (NSUBW - NSUB)
                ri = i - (NSUBW - NSUB)
                p.dma("gpsimd", sX[buf], t_x[buf][:],
                      x_in[i * 128:(i + 1) * 128, :], w=[K("t_x", buf)])
                ln_subtile(buf, K("t_x", buf))
                cop("scalar", lambda e, b=buf: e.activation(
                    out=t_xnb[b][:], in_=t_x[b][:], func=AF.Identity,
                    scale=cl[b]["rstd"][:], bias=cl[b]["negmr"][:]),
                    r=[K("t_x", buf), K("cl", buf, "rstd"),
                       K("cl", buf, "negmr")],
                    w=[K("t_xnb", buf)])
                p.dma("scalar", sXN[buf], xnbuf[i * 128:(i + 1) * 128, :],
                      t_xnb[buf][:], r=[K("t_xnb", buf)], w=[K("xnbuf", i)])
                if real:
                    b_ = buf
                    clb_ = cl[b_]
                    cop("vector", lambda e, b_=b_, clb_=clb_: e.reduce_max(
                        out=clb_["amax"][:], in_=t_xn[b_][:], axis=AX.X,
                        apply_absolute_value=True),
                        r=[K("t_xn", b_)], w=[K("cl", b_, "amax")])
                    cop("vector", lambda e, b_=b_, clb_=clb_: e.tensor_scalar(
                        out=clb_["am1c"][:], in0=clb_["amax"][:], scalar1=1e-5,
                        scalar2=None, op0=ALU.max),
                        r=[K("cl", b_, "amax")], w=[K("cl", b_, "am1c")])
                    cop("vector", lambda e, r_=ri, clb_=clb_: e.tensor_copy(
                        out=amax1_all[:, r_:r_ + 1], in_=clb_["am1c"][:]),
                        r=[K("cl", b_, "am1c")], w=[K("amax1", ri)])
                    cop("vector", lambda e, clb_=clb_: e.reciprocal(
                        out=clb_["rec"][:], in_=clb_["am1c"][:]),
                        r=[K("cl", b_, "am1c")], w=[K("cl", b_, "rec")])
                    cop("vector", lambda e, clb_=clb_: e.tensor_scalar(
                        out=clb_["s1"][:], in0=clb_["rec"][:], scalar1=127.0,
                        scalar2=None, op0=ALU.mult),
                        r=[K("cl", b_, "rec")], w=[K("cl", b_, "s1")])
                    cop("vector", lambda e, b_=b_, clb_=clb_: e.tensor_scalar(
                        out=t_q1m[:], in0=t_xn[b_][:],
                        scalar1=clb_["s1"][:],
                        scalar2=MAGIC, op0=ALU.mult, op1=ALU.add),
                        r=[K("t_xn", b_), K("cl", b_, "s1")],
                        w=[K("t_q1m")])
                    cop("vector", lambda e, b_=b_: e.tensor_scalar(
                        out=t_q1m[:], in0=t_q1m[:], scalar1=MAGIC,
                        scalar2=127.0, op0=ALU.subtract, op1=ALU.min),
                        r=[K("t_q1m")], w=[K("t_q1m")])
                    cop("vector", lambda e, b=buf: e.tensor_scalar(
                        out=t_q1b[b][:], in0=t_q1m[:], scalar1=-128.0,
                        scalar2=None, op0=ALU.max),
                        r=[K("t_q1m")], w=[K("t_q1b", buf)])
                    p.dma("scalar", sQ1[buf],
                          q1buf[ri * 128:(ri + 1) * 128, :],
                          t_q1b[buf][:], r=[K("t_q1b", buf)],
                          w=[K("q1buf", ri)])

            def p1_mean_chunk(c):
                b = c % 2
                if c < W1CH:
                    src = w1_in[c * 128:(c + 1) * 128, :]
                    width = DM
                else:
                    src = w2_in[(c - W1CH) * 128:(c - W1CH + 1) * 128, :]
                    width = DFF
                p.dma("gpsimd", sWB[b], wb[b][:, 0:width], src, w=[K("wb", b)])
                cop("scalar", lambda e, b=b, c=c, width=width: e.activation(
                    out=wf[b][:, 0:width], in_=wb[b][:, 0:width], func=AF.Abs,
                    accum_out=wacc[:, c:c + 1]),
                    r=[K("wb", b)], w=[K("wf", b), K("wacc", c)])

            def p1_reduce():
                cop("vector", lambda e: e.reduce_sum(
                    out=rs2[:, 0:1], in_=wacc[:, 0:W1CH], axis=AX.X),
                    r=[K("wacc", c) for c in range(W1CH)], w=[K("rs2", 0)])
                cop("vector", lambda e: e.reduce_sum(
                    out=rs2[:, 1:2], in_=wacc[:, W1CH:W1CH + W2CH], axis=AX.X),
                    r=[K("wacc", W1CH + c) for c in range(W2CH)],
                    w=[K("rs2", 1)])
                cop("tensor", lambda e: e.matmul(
                    out=pm11[:], lhsT=c_ones_colf[:], rhs=rs2[:],
                    start=True, stop=True),
                    r=[K("rs2", 0), K("rs2", 1), K("c", 4)], w=[K("pm11")])
                cop("vector", lambda e: e.tensor_scalar(
                    out=m12s[:], in0=pm11[:], scalar1=1.0 / NELW, scalar2=1e-5,
                    op0=ALU.mult, op1=ALU.max),
                    r=[K("pm11")], w=[K("m12s")])
                cop("tensor", lambda e: e.matmul(
                    out=pm128[:], lhsT=c_ones_rowf[:], rhs=m12s[:],
                    start=True, stop=True),
                    r=[K("m12s"), K("c", 3)], w=[K("pm128")])
                cop("vector", lambda e: e.tensor_copy(
                    out=c_mean12[:], in_=pm128[:]),
                    r=[K("pm128")], w=[K("cm12")])
                cop("vector", lambda e: e.reciprocal(
                    out=c_ws12[:], in_=c_mean12[:]),
                    r=[K("cm12")], w=[K("cws12")])

            def p1_quant_chunk(c):
                b = c % 2
                if c < W1CH:
                    src = w1_in[c * 128:(c + 1) * 128, :]
                    dst = t1q[c * 128:(c + 1) * 128, :]
                    width = DM
                    wcol = 0
                else:
                    src = w2_in[(c - W1CH) * 128:(c - W1CH + 1) * 128, :]
                    dst = t2q[(c - W1CH) * 128:(c - W1CH + 1) * 128, :]
                    width = DFF
                    wcol = 1
                p.dma("sync", sWF[b], wf[b][:, 0:width], src, w=[K("wf", b)])
                cop("vector", lambda e, b=b, width=width, wcol=wcol:
                    e.tensor_scalar(
                        out=wf[b][:, 0:width], in0=wf[b][:, 0:width],
                        scalar1=c_ws12[:, wcol:wcol + 1], scalar2=MAGIC,
                        op0=ALU.mult, op1=ALU.add),
                    r=[K("wf", b), K("cws12")], w=[K("wf", b)])
                cop("scalar", lambda e, b=b, width=width: e.activation(
                    out=wf[b][:, 0:width], in_=wf[b][:, 0:width],
                    func=AF.Identity, bias=c_negmag[:]),
                    r=[K("wf", b), K("c", "nm")], w=[K("wf", b)])
                cop("vector", lambda e, b=b, width=width: e.tensor_scalar(
                    out=wb[b][:, 0:width], in0=wf[b][:, 0:width],
                    scalar1=-1.0, scalar2=1.0, op0=ALU.max, op1=ALU.min),
                    r=[K("wf", b)], w=[K("wb", b)])
                p.dma("scalar", sWQ[b], dst, wb[b][:, 0:width],
                      r=[K("wb", b)], w=[K("tq", c)])

            NCH = W1CH + W2CH
            mc = 0
            for i in range(NSUBW):
                p3_subtile(i)
                want = (i + 1) * NCH // NSUBW
                while mc < want:
                    p1_mean_chunk(mc)
                    mc += 1
            while mc < NCH:
                p1_mean_chunk(mc)
                mc += 1
            p1_reduce()
            cop("vector", lambda e: e.tensor_scalar(
                out=evs1_all[:], in0=amax1_all[:], scalar1=c_mean12[:, 0:1],
                scalar2=1.0 / 127.0, op0=ALU.mult, op1=ALU.mult),
                r=[K("cm12")] + [K("amax1", r_) for r_ in range(NSUB)],
                w=[K("evs1")])
            for c in range(NCH):
                p1_quant_chunk(c)
            p.barrier()

        with ExitStack() as c4:
            c_wxb = sb("c_wxb", (128, NK * NSSM), BF16, c=c4)
            db_T = sb("db_T", (NH, NTW), BF16, c=c4)
            for k in range(NK):
                p.dma("gpsimd", sCg, c_wxb[:, k * NSSM:(k + 1) * NSSM],
                      wx_in[k * 128:(k + 1) * 128, :], w=[K("c", "wx", k)])
            sxnt = sb("sxnt", (128, NK * P4SLAB), BF16, c=c4)
            t_ssm = [sb(f"t_ssm{i}", (128, NSSM), BF16, c=c4) for i in range(2)]
            pwx = [ps(f"pwx{i}", (128, NSSM), c=c4) for i in range(2)]
            pdb = ps("pdb", (NH, 128), BF16, c=c4)
            sT4 = sem("sT4")
            sM4 = sem("sM4")
            nsub_slab = P4SLAB // SUB
            for slab in range(4):
                for k in range(NK):
                    p.dmat("sync", sT4, sxnt[:, k * P4SLAB:(k + 1) * P4SLAB],
                           xnbuf[slab * P4SLAB:(slab + 1) * P4SLAB,
                                 k * 128:(k + 1) * 128],
                           r=[K("xnbuf", slab * nsub_slab + j)
                              for j in range(nsub_slab)],
                           w=[K("sxnt", k)])
                for j in range(nsub_slab):
                    i = slab * nsub_slab + j
                    pb = i % 2
                    for k in range(NK):
                        fn = (lambda e, k=k, j=j, pb=pb: e.matmul(
                            out=pwx[pb][:],
                            lhsT=sxnt[:, k * P4SLAB + j * 128:
                                      k * P4SLAB + (j + 1) * 128],
                            rhs=c_wxb[:, k * NSSM:(k + 1) * NSSM],
                            start=(k == 0), stop=(k == NK - 1)))
                        if k < NK - 1:
                            copq("tensor", fn,
                                 r=[K("sxnt", k), K("c", "wx", k)],
                                 w=([K("pwx", pb)] if k == 0 else []))
                        else:
                            cop("tensor", fn,
                                r=[K("sxnt", kk) for kk in range(NK)] +
                                  [K("c", "wx", kk) for kk in range(NK)],
                                w=[K("pwx", pb)])
                    if i < NWARM // SUB:
                        cop("vector", lambda e, pb=pb: e.tensor_scalar(
                            out=t_ssm[pb][:], in0=pwx[pb][:],
                            scalar1=c_woff[:], scalar2=None, op0=ALU.add),
                            r=[K("pwx", pb), K("c", 8)], w=[K("t_ssm", pb)])
                    else:
                        cop("scalar", lambda e, pb=pb: e.activation(
                            out=t_ssm[pb][:], in_=pwx[pb][:], func=AF.Identity),
                            r=[K("pwx", pb)], w=[K("t_ssm", pb)])
                    p.dma("scalar", sM4, ssmbuf[i * 128:(i + 1) * 128, :],
                          t_ssm[pb][:], r=[K("t_ssm", pb)], w=[K("ssm", i)])
                    cop("tensor", lambda e, pb=pb: e.transpose(
                        out=pdb[:], in_=t_ssm[pb][:, 2 * NS:2 * NS + NH],
                        identity=c_identb[:]),
                        r=[K("t_ssm", pb), K("c", 5)], w=[K("pdb")])
                    cop("scalar", lambda e, i=i: e.activation(
                        out=db_T[:, i * 128:(i + 1) * 128], in_=pdb[:],
                        func=AF.Identity),
                        r=[K("pdb")], w=[K("dbT", i)])

            bB = [sb(f"bB{i}", (128, 512), BF16, c=c4) for i in range(2)]
            fB = [sb(f"fB{i}", (128, 512), c=c4) for i in range(2)]
            bC = [sb(f"bC{i}", (128, 512), BF16, c=c4) for i in range(2)]
            fC = [sb(f"fC{i}", (128, 512), c=c4) for i in range(2)]
            t_zt = [sb(f"t_zt{i}", (128, 512), c=c4) for i in range(2)]
            t_e = [sb(f"t_e{i}", (128, 512), c=c4) for i in range(2)]
            t_d5 = [sb(f"t_d5{i}", (128, 512), c=c4) for i in range(2)]
            t_a5 = [sb(f"t_a5{i}", (128, 512), c=c4) for i in range(2)]
            t_u5 = [sb(f"t_u5{i}", (128, 512), c=c4) for i in range(2)]
            t_hs = [sb(f"t_hs{i}", (128, 512), c=c4) for i in range(2)]
            t_mb = [sb(f"t_mb{i}", (128, 512), BF16, c=c4) for i in range(2)]
            st8 = sb("st8", (128, NH), c=c4)
            t_yrow = [sb(f"t_yrow{i}", (1, 512), c=c4) for i in range(2)]
            t_yb32 = sb("t_yb32", (NH, 128), c=c4)
            pz = [ps(f"pz{i}", (128, 512), c=c4) for i in range(2)]
            py = [ps(f"py{i}", (1, 512), c=c4) for i in range(2)]
            pytr = ps("pytr", (128, NH), c=c4)
            sB5 = [sem("sB5a"), sem("sB5b")]
            sC5 = [sem("sC5a"), sem("sC5b")]
            sYR = [sem("sYR0"), sem("sYR1")]
            sYL = sem("sYL")
            tt = [sb(f"tt{i}", (128, DFF), BF16, c=c4) for i in range(2)]
            sTT = [sem("sTT0"), sem("sTT1")]
            sTB = [sem("sTB0"), sem("sTB1")]
            tt2 = [sb(f"tt2_{i}", (128, DM), BF16, c=c4) for i in range(2)]
            sT2T = [sem("sT2T0"), sem("sT2T1")]
            sT2B = [sem("sT2B0"), sem("sT2B1")]

            def p2t1_unit(d):
                b = d % 2
                p.dmat("sync", sTT[b], tt[b][:, 0:DFF],
                       t1q[0:DFF, d * 128:(d + 1) * 128],
                       r=[K("tq", c) for c in range(W1CH)], w=[K("tt", b)])
                for n in range(NSL1):
                    p.dma("gpsimd", sTB[b], t1b[n, :, d, :],
                          tt[b][:, n * 512:(n + 1) * 512],
                          r=[K("tt", b)], w=[K("t1b", n, d)])

            def p2t2_unit(f):
                b2 = f % 2
                p.dmat("sync", sT2T[b2], tt2[b2][:, 0:DM],
                       t2q[0:DM, f * 128:(f + 1) * 128],
                       w=[K("tt2", b2)])
                for n in range(NSL2):
                    p.dma("gpsimd", sT2B[b2], t2b[n, :, f, :],
                          tt2[b2][:, n * 512:(n + 1) * 512],
                          r=[K("tt2", b2)], w=[K("t2b", n, f)])

            p2_thunks = ([(lambda d=d: p2t1_unit(d)) for d in range(NK)] +
                         [(lambda f=f: p2t2_unit(f)) for f in range(NKF)])

            def issue_chunk_loads(tc):
                cb = tc % 2
                p.dmat("sync", sB5[cb], bB[cb][:],
                       ssmbuf[tc * 512:(tc + 1) * 512, 0:NS],
                       r=[K("ssm", tc * 4 + j) for j in range(4)],
                       w=[K("bB", cb)])
                if tc >= NWARM // 512:
                    p.dmat("sync", sC5[cb], bC[cb][:],
                           ssmbuf[tc * 512:(tc + 1) * 512, NS:2 * NS],
                           r=[K("ssm", tc * 4 + j) for j in range(4)],
                           w=[K("bC", cb)])

            issue_chunk_loads(0)
            for tc in range(NCHUNK):
                cb = tc % 2
                realc = tc >= NWARM // 512
                rc = tc - NWARM // 512
                if tc + 1 < NCHUNK:
                    issue_chunk_loads(tc + 1)
                cop("vector", lambda e, cb=cb: e.tensor_copy(
                    out=fB[cb][:], in_=bB[cb][:]),
                    r=[K("bB", cb)], w=[K("fB", cb)])
                if realc:
                    cop("vector", lambda e, cb=cb: e.tensor_copy(
                        out=fC[cb][:], in_=bC[cb][:]),
                        r=[K("bC", cb)], w=[K("fC", cb)])
                def scan_head(h, tc=tc, cb=cb):
                    pb = h % 2
                    cop("tensor", lambda e, h=h, tc=tc, pb=pb: e.matmul(
                        out=pz[pb][:], lhsT=c_sel[:, h * 128:(h + 1) * 128],
                        rhs=db_T[:, tc * 512:(tc + 1) * 512],
                        start=True, stop=True),
                        r=[K("dbT", tc * 4), K("dbT", tc * 4 + 1),
                           K("dbT", tc * 4 + 2), K("dbT", tc * 4 + 3),
                           K("c", 11)],
                        w=[K("pz", pb)])
                    cop("vector", lambda e, pb=pb, cb=cb: e.tensor_tensor(
                        out=t_zt[pb][:], in0=pz[pb][:], in1=fB[cb][:],
                        op=ALU.add),
                        r=[K("pz", pb), K("fB", cb)], w=[K("zt", pb)])
                    cop("scalar", lambda e, pb=pb: e.activation(
                        out=t_e[pb][:], in_=t_zt[pb][:], func=AF.Exp),
                        r=[K("zt", pb)], w=[K("te", pb)])
                    cop("scalar", lambda e, pb=pb: e.activation(
                        out=t_d5[pb][:], in_=t_e[pb][:], func=AF.Ln, bias=1.0),
                        r=[K("te", pb)], w=[K("td", pb)])
                    cop("scalar", lambda e, h=h, pb=pb: e.activation(
                        out=t_a5[pb][:], in_=t_d5[pb][:], func=AF.Exp,
                        scale=c_at[:, h:h + 1]),
                        r=[K("td", pb), K("c", 7)], w=[K("ta", pb)])
                    cop("scalar", lambda e, pb=pb: e.activation(
                        out=t_u5[pb][:], in_=t_d5[pb][:], func=AF.Square),
                        r=[K("td", pb)], w=[K("tu", pb)])

                def scan_tail(h, tc=tc, cb=cb, rc=rc, realc=realc):
                    pb = h % 2
                    if tc == 0:
                        cop("vector", lambda e, pb=pb: e.tensor_tensor_scan(
                            out=t_hs[pb][:], data0=t_a5[pb][:],
                            data1=t_u5[pb][:],
                            initial=0.0, op0=ALU.mult, op1=ALU.add),
                            r=[K("ta", pb), K("tu", pb)], w=[K("hs", pb)])
                    else:
                        cop("vector", lambda e, h=h, pb=pb:
                            e.tensor_tensor_scan(
                                out=t_hs[pb][:], data0=t_a5[pb][:],
                                data1=t_u5[pb][:],
                                initial=st8[:, h:h + 1], op0=ALU.mult,
                                op1=ALU.add),
                            r=[K("ta", pb), K("tu", pb), K("st8", h)],
                            w=[K("hs", pb)])
                    cop("vector", lambda e, h=h, pb=pb: e.tensor_copy(
                        out=st8[:, h:h + 1], in_=t_hs[pb][:, 511:512]),
                        r=[K("hs", pb)], w=[K("st8", h)])
                    if realc:
                        cop("vector", lambda e, pb=pb, cb=cb: e.tensor_tensor(
                            out=t_mb[pb][:], in0=t_hs[pb][:], in1=fC[cb][:],
                            op=ALU.mult),
                            r=[K("hs", pb), K("fC", cb)], w=[K("mb", pb)])
                        cop("tensor", lambda e, pb=pb: e.matmul(
                            out=py[pb][:], lhsT=c_ones_col[:],
                            rhs=t_mb[pb][:],
                            start=True, stop=True),
                            r=[K("mb", pb), K("c", 2)], w=[K("py", pb)])
                        yb = h % 2
                        cop("vector", lambda e, pb=pb, yb=yb: e.tensor_copy(
                            out=t_yrow[yb][:], in_=py[pb][:]),
                            r=[K("py", pb)], w=[K("t_yrow", yb)])
                        p.dma("gpsimd", sYR[yb],
                              ybuf[h:h + 1, rc * 512:(rc + 1) * 512],
                              t_yrow[yb][:], r=[K("t_yrow", yb)],
                              w=[K("ybuf", rc, h)])

                for h in range(NH):
                    scan_head(h)
                    if h >= 1:
                        scan_tail(h - 1)
                    if p2_thunks and h % 3 == 1:
                        p2_thunks.pop(0)()
                scan_tail(NH - 1)
            while p2_thunks:
                p2_thunks.pop(0)()
            for s_ in range(NSUB):
                p.dma("gpsimd", sYL, t_yb32[:],
                      ybuf[:, s_ * 128:(s_ + 1) * 128],
                      r=[K("ybuf", s_ // 4, h) for h in range(NH)],
                      w=[K("t_yb32")])
                cop("tensor", lambda e: e.transpose(
                    out=pytr[:], in_=t_yb32[:],
                    identity=c_ident32f[0:NH, 0:NH]),
                    r=[K("t_yb32"), K("c", 6)], w=[K("pytr")])
                cop("vector", lambda e, s_=s_: e.tensor_copy(
                    out=y_all[:, s_ * NH:(s_ + 1) * NH], in_=pytr[:]),
                    r=[K("pytr")], w=[K("yall", s_)])
            p.barrier()
        for s_ in range(NSUP):
            nsub_s = TSUP // SUB
            with ExitStack() as c6:
                q1T = sb(f"q1T{s_}", (128, NK * TSUP), BF16, c=c6)
                rhs1 = [sb(f"rhs1_{s_}_{i}", (128, NK * 512), BF16, c=c6)
                        for i in range(2)]
                t_h = [sb(f"t_h{s_}_{i}", (128, 512), c=c6) for i in range(4)]
                cpam = sb(f"cpam{s_}", (128, 1), c=c6)
                pw1 = [ps(f"pw1_{s_}_{i}", (128, 512), c=c6) for i in range(4)]
                sQT = sem(f"sQT{s_}")
                sR1 = [sem(f"sR1a{s_}"), sem(f"sR1b{s_}")]
                sH = [sem(f"sH{s_}_{i}") for i in range(4)]
                for k in range(NK):
                    p.dmat("sync", sQT, q1T[:, k * TSUP:(k + 1) * TSUP],
                           q1buf[s_ * TSUP:(s_ + 1) * TSUP,
                                 k * 128:(k + 1) * 128],
                           r=[K("q1buf", s_ * nsub_s + j)
                              for j in range(nsub_s)],
                           w=[K("q1T", k)])
                for n in range(NSL1):
                    rb = n % 2
                    p.dma("gpsimd", sR1[rb], rhs1[rb][:],
                          t1b[n].rearrange("p k f -> p (k f)"),
                          r=[K("t1b", n, d) for d in range(NK)],
                          w=[K("rhs1", rb)])
                    for j in range(nsub_s):
                        pb = j % 4
                        for k in range(NK):
                            fn = (lambda e, k=k, j=j, pb=pb, rb=rb,
                                  pw1=pw1, q1T=q1T, rhs1=rhs1:
                                  e.matmul(
                                      out=pw1[pb][:],
                                      lhsT=q1T[:, k * TSUP + j * 128:
                                               k * TSUP + (j + 1) * 128],
                                      rhs=rhs1[rb][:, k * 512:(k + 1) * 512],
                                      start=(k == 0), stop=(k == NK - 1)))
                            if k < NK - 1:
                                copq("tensor", fn,
                                     r=[K("q1T", k), K("rhs1", rb)],
                                     w=([K("pw1", pb)] if k == 0 else []))
                            else:
                                cop("tensor", fn,
                                    r=[K("q1T", kk) for kk in range(NK)] +
                                      [K("rhs1", rb)],
                                    w=[K("pw1", pb)])
                        cop("scalar", lambda e, pb=pb, j=j, s_=s_,
                            t_h=t_h, pw1=pw1: e.activation(
                            out=t_h[pb][:], in_=pw1[pb][:], func=AF.Silu,
                            scale=evs1_all[:, s_ * nsub_s + j:
                                           s_ * nsub_s + j + 1]),
                            r=[K("pw1", pb), K("evs1")], w=[K("t_h", pb)])
                        cop("vector", lambda e, pb=pb, cpam=cpam, t_h=t_h:
                            e.reduce_max(
                            out=cpam[:], in_=t_h[pb][:], axis=AX.X,
                            apply_absolute_value=True),
                            r=[K("t_h", pb)], w=[K("cpam")])
                        if n == 0:
                            cop("vector", lambda e, j=j, cpam=cpam: e.tensor_copy(
                                out=amax2_all[:, j:j + 1], in_=cpam[:]),
                                r=[K("cpam")], w=[K("amax2", j)])
                        else:
                            cop("vector", lambda e, j=j, cpam=cpam: e.tensor_tensor(
                                out=amax2_all[:, j:j + 1],
                                in0=amax2_all[:, j:j + 1], in1=cpam[:],
                                op=ALU.max),
                                r=[K("cpam"), K("amax2", j)],
                                w=[K("amax2", j)])
                        p.dma("scalar", sH[pb],
                              hbuf[j * 128:(j + 1) * 128,
                                   n * 512:(n + 1) * 512],
                              t_h[pb][:], r=[K("t_h", pb)],
                              w=[K("hbuf", j, n)])
                p.barrier()
            with ExitStack() as c7:
                GT = 256
                NGRP = TSUP // GT
                NQ = NKF // 8
                q2Tg = [sb(f"q2Tg{s_}_{i}", (128, NKF * GT), BF16, c=c7)
                        for i in range(2)]
                rhs2 = [sb(f"rhs2_{s_}_{i}", (128, NQ * 512), BF16, c=c7)
                        for i in range(2)]
                t_hr = sb(f"t_hr{s_}", (128, DFF // 2), c=c7)
                t_q2b = [sb(f"t_q2b{s_}_{i}", (128, DFF), BF16, c=c7)
                         for i in range(4)]
                t_g = [sb(f"t_g{s_}_{i}", (128, 512), c=c7) for i in range(2)]
                cq = {nm: sb(f"cq{s_}_" + nm, (128, 1), c=c7) for nm in
                      ("am2c", "rec", "s2", "mag")}
                pq2 = [ps(f"pq2_{s_}_{i}", (128, 512), BF16, c=c7)
                       for i in range(2)]
                pw2 = [ps(f"pw2_{s_}_{i}", (128, 512), c=c7) for i in range(2)]
                sHR = sem(f"sHR{s_}")
                sR2 = [sem(f"sR2a{s_}"), sem(f"sR2b{s_}")]
                sG = [sem(f"sGa{s_}"), sem(f"sGb{s_}")]
                HH = DFF // 2
                cop("vector", lambda e, cq=cq: e.memset(cq["mag"][:], MAGIC),
                    w=[K("cq", "mag")])

                def p7_quant(j):
                    jb = j % 4
                    cop("vector", lambda e, j=j, cq=cq: e.tensor_scalar(
                        out=cq["am2c"][:], in0=amax2_all[:, j:j + 1],
                        scalar1=1e-5, scalar2=None, op0=ALU.max),
                        r=[K("amax2", j)], w=[K("cq", "am2c")])
                    cop("vector", lambda e, cq=cq: e.reciprocal(
                        out=cq["rec"][:], in_=cq["am2c"][:]),
                        r=[K("cq", "am2c")], w=[K("cq", "rec")])
                    cop("vector", lambda e, cq=cq: e.tensor_scalar(
                        out=cq["s2"][:], in0=cq["rec"][:], scalar1=127.0,
                        scalar2=None, op0=ALU.mult),
                        r=[K("cq", "rec")], w=[K("cq", "s2")])
                    cop("vector", lambda e, j=j, cq=cq: e.tensor_scalar(
                        out=evs2_all[:, j:j + 1], in0=cq["am2c"][:],
                        scalar1=c_mean12[:, 1:2], scalar2=1.0 / 127.0,
                        op0=ALU.mult, op1=ALU.mult),
                        r=[K("cq", "am2c"), K("cm12")], w=[K("evs2", j)])
                    for hh in range(2):
                        p.dma("scalar", sHR, t_hr[:],
                              hbuf[j * 128:(j + 1) * 128,
                                   hh * HH:(hh + 1) * HH],
                              r=[K("hbuf", j, n) for n in range(NSL1)],
                              w=[K("t_hr")])
                        cop("scalar", lambda e, t_hr=t_hr, cq=cq:
                            e.activation(
                                out=t_hr[:], in_=t_hr[:],
                                func=AF.Identity, scale=cq["s2"][:],
                                bias=cq["mag"][:]),
                            r=[K("t_hr"), K("cq", "s2"), K("cq", "mag")],
                            w=[K("t_hr")])
                        cop("vector", lambda e, t_hr=t_hr: e.tensor_scalar(
                            out=t_hr[:], in0=t_hr[:],
                            scalar1=MAGIC, scalar2=127.0,
                            op0=ALU.subtract, op1=ALU.min),
                            r=[K("t_hr")], w=[K("t_hr")])
                        cop("vector", lambda e, hh=hh, jb=jb,
                            t_q2b=t_q2b, t_hr=t_hr: e.tensor_scalar(
                                out=t_q2b[jb][:, hh * HH:(hh + 1) * HH],
                                in0=t_hr[:],
                                scalar1=-128.0, scalar2=None, op0=ALU.max),
                            r=[K("t_hr")], w=[K("t_q2b", jb, hh)])

                def p7_grp_tr(grp):
                    gb = grp % 2
                    for s2_ in range(2):
                        j = grp * 2 + s2_
                        jb = j % 4
                        for kg in range(NKF // 4):
                            pb = kg % 2
                            for kk in range(4):
                                k = kg * 4 + kk
                                fn = (lambda e, k=k, kk=kk, pb=pb, jb=jb,
                                      pq2=pq2, t_q2b=t_q2b: e.transpose(
                                          out=pq2[pb][:, kk * 128:
                                                      (kk + 1) * 128],
                                          in_=t_q2b[jb][:, k * 128:
                                                        (k + 1) * 128],
                                          identity=c_identb[:]))
                                if kk == 0:
                                    copq("tensor", fn,
                                         r=[K("t_q2b", jb, k // (NKF // 2)),
                                            K("c", 5)],
                                         w=[K("pq2", pb)])
                                elif kk < 3:
                                    copq("tensor", fn,
                                         r=[K("t_q2b", jb,
                                              k // (NKF // 2)), K("c", 5)])
                                else:
                                    cop("tensor", fn,
                                        r=[K("t_q2b", jb, 0),
                                           K("t_q2b", jb, 1), K("c", 5)],
                                        w=[K("pq2", pb)])
                            cop("vector", lambda e, kg=kg, s2_=s2_, gb=gb,
                                pb=pb, q2Tg=q2Tg, pq2=pq2:
                                e.tensor_copy(
                                    out=q2Tg[gb][:].rearrange(
                                        "p (k t) -> p k t", t=GT)[
                                        :, 4 * kg:4 * kg + 4,
                                        s2_ * 128:(s2_ + 1) * 128],
                                    in_=pq2[pb][:].rearrange(
                                        "p (k t) -> p k t", t=128)),
                                r=[K("pq2", pb)], w=[K("q2Tg", gb, kg)])

                def p7_grp_mms(grp):
                    gb = grp % 2
                    for n in range(NSL2):
                        for q in range(8):
                            rb = q % 2
                            p.dma("gpsimd", sR2[rb], rhs2[rb][:],
                                  t2b[n, :, q * NQ:(q + 1) * NQ, :]
                                  .rearrange("p k f -> p (k f)"),
                                  r=[K("t2b", n, f) for f in
                                     range(q * NQ, (q + 1) * NQ)],
                                  w=[K("rhs2", rb)])
                            for s2_ in range(2):
                                for kk in range(NQ):
                                    k = q * NQ + kk
                                    fn = (lambda e, k=k, kk=kk, rb=rb, s2_=s2_,
                                          gb=gb, pw2=pw2, q2Tg=q2Tg, rhs2=rhs2:
                                          e.matmul(
                                              out=pw2[s2_][:],
                                              lhsT=q2Tg[gb][:,
                                                           k * GT + s2_ * 128:
                                                           k * GT +
                                                           (s2_ + 1) * 128],
                                              rhs=rhs2[rb][:, kk * 512:
                                                           (kk + 1) * 512],
                                              start=(k == 0),
                                              stop=(k == NKF - 1)))
                                    if k == NKF - 1:
                                        cop("tensor", fn,
                                            r=[K("q2Tg", gb, kk2)
                                               for kk2 in range(NKF // 4)] +
                                              [K("rhs2", 0), K("rhs2", 1)],
                                            w=[K("pw2", s2_)])
                                    elif k == 0:
                                        cop("tensor", fn,
                                            r=[K("q2Tg", gb, 0),
                                               K("rhs2", rb)],
                                            w=[K("pw2", s2_)])
                                    elif kk == NQ - 1:
                                        cop("tensor", fn,
                                            r=[K("q2Tg", gb, k // 4),
                                               K("rhs2", rb)], w=[])
                                    else:
                                        copq("tensor", fn,
                                             r=[K("q2Tg", gb, k // 4),
                                                K("rhs2", rb)])
                        for s2_ in range(2):
                            j = grp * 2 + s2_
                            cop("scalar", lambda e, s2_=s2_, j=j,
                                t_g=t_g, pw2=pw2: e.activation(
                                out=t_g[s2_][:], in_=pw2[s2_][:],
                                func=AF.Silu,
                                scale=evs2_all[:, j:j + 1]),
                                r=[K("pw2", s2_), K("evs2", j)],
                                w=[K("t_g", s2_)])
                            p.dma("scalar", sG[s2_],
                                  gbuf[(s_ * TSUP + j * 128):
                                       (s_ * TSUP + (j + 1) * 128),
                                       n * 512:(n + 1) * 512],
                                  t_g[s2_][:], r=[K("t_g", s2_)],
                                  w=[K("gbuf", s_ * (TSUP // SUB) + j, n)])

                for j in range(min(4, TSUP // SUB)):
                    p7_quant(j)
                p7_grp_tr(0)
                for grp in range(NGRP):
                    if grp + 1 < NGRP:
                        p7_grp_tr(grp + 1)
                    for j in range(2 * (grp + 2), 2 * (grp + 3)):
                        if j < TSUP // SUB:
                            p7_quant(j)
                    p7_grp_mms(grp)
                p.barrier()

        with ExitStack() as cb1:
            c_dp = sb("c_dp", (128, DM), c=cb1)
            p.dma("sync", sC, c_dp[:], dp_in, w=[K("c", 9)])
            wgset = sb("wgset", (128, NK * DM), BF16, c=cb1)
            xnT = sb("xnT", (128, NK * 512), BF16, c=cb1)
            t_xb = [sb(f"t_xb{i}", (128, DM), c=cb1) for i in range(2)]
            t_sqb = sb("t_sqb", (128, DM), c=cb1)
            t_xn1 = [sb(f"t_xn1{i}", (128, DM), c=cb1) for i in range(2)]
            t_gate = [sb(f"t_gate{i}", (128, DM), c=cb1) for i in range(2)]
            t_yh = [sb(f"t_yh{i}", (128, DM), c=cb1) for i in range(2)]
            t_yfb = [sb(f"t_yfb{i}", (128, DM), BF16, c=cb1) for i in range(2)]
            clb = [{nm: sb(f"clb{i}_" + nm, (128, 1), c=cb1) for nm in
                    ("sum", "sq", "m", "m2", "v", "rstd", "negmr")}
                   for i in range(2)]
            pg = [ps(f"pg{i}", (128, 512), c=cb1) for i in range(8)]
            sWG = sem("sWG")
            sXB = [sem("sXB0"), sem("sXB1")]
            sXT = sem("sXT")
            sYF = [sem("sYF0"), sem("sYF1")]
            for k in range(NK):
                p.dma("gpsimd", sWG, wgset[:, k * DM:(k + 1) * DM],
                      wg_in[k * 128:(k + 1) * 128, :], w=[K("wgset", k)])

            def ln_b(t_xt, t_sqt, clx, keyx, t_out, keyout, kb):
                cop("vector", lambda e, t_xt=t_xt, clx=clx: e.reduce_sum(
                    out=clx["sum"][:], in_=t_xt[:], axis=AX.X),
                    r=[keyx], w=[K("clb", kb, "sum")])
                cop("scalar", lambda e, t_xt=t_xt, t_sqt=t_sqt, clx=clx:
                    e.activation(
                    out=t_sqt[:], in_=t_xt[:], func=AF.Square,
                    accum_out=clx["sq"][:]),
                    r=[keyx], w=[K("tsqb"), K("clb", kb, "sq")])
                cop("vector", lambda e, clx=clx: e.tensor_scalar(
                    out=clx["m"][:], in0=clx["sum"][:], scalar1=1.0 / DM,
                    scalar2=None, op0=ALU.mult),
                    r=[K("clb", kb, "sum")], w=[K("clb", kb, "m")])
                cop("vector", lambda e, clx=clx: e.tensor_tensor(
                    out=clx["m2"][:], in0=clx["m"][:], in1=clx["m"][:],
                    op=ALU.mult), r=[K("clb", kb, "m")], w=[K("clb", kb, "m2")])
                cop("vector", lambda e, clx=clx: e.tensor_scalar(
                    out=clx["v"][:], in0=clx["sq"][:], scalar1=1.0 / DM,
                    scalar2=None, op0=ALU.mult),
                    r=[K("clb", kb, "sq")], w=[K("clb", kb, "v")])
                cop("vector", lambda e, clx=clx: e.tensor_tensor(
                    out=clx["v"][:], in0=clx["v"][:], in1=clx["m2"][:],
                    op=ALU.subtract), r=[K("clb", kb, "m2"), K("clb", kb, "v")],
                    w=[K("clb", kb, "v")])
                cop("scalar", lambda e, clx=clx: e.activation(
                    out=clx["rstd"][:], in_=clx["v"][:], func=AF.Ln,
                    bias=c_eps[:]),
                    r=[K("clb", kb, "v"), K("c", "eps")],
                    w=[K("clb", kb, "rstd")])
                cop("scalar", lambda e, clx=clx: e.activation(
                    out=clx["rstd"][:], in_=clx["rstd"][:], func=AF.Exp,
                    scale=-0.5), r=[K("clb", kb, "rstd")],
                    w=[K("clb", kb, "rstd")])
                cop("vector", lambda e, clx=clx: e.tensor_scalar(
                    out=clx["negmr"][:], in0=clx["m"][:],
                    scalar1=clx["rstd"][:], scalar2=-1.0, op0=ALU.mult,
                    op1=ALU.mult),
                    r=[K("clb", kb, "m"), K("clb", kb, "rstd")],
                    w=[K("clb", kb, "negmr")])
                cop("scalar", lambda e, t_xt=t_xt, t_out=t_out, clx=clx:
                    e.activation(
                    out=t_out[:], in_=t_xt[:], func=AF.Identity,
                    scale=clx["rstd"][:], bias=clx["negmr"][:]),
                    r=[keyx, K("clb", kb, "rstd"), K("clb", kb, "negmr")],
                    w=[keyout])

            for sub in range(NSUB):
                buf = sub % 2
                if sub % 4 == 0:
                    grp = sub // 4
                    for k in range(NK):
                        p.dmat("sync", sXT, xnT[:, k * 512:(k + 1) * 512],
                               xnbuf[NWARM + grp * 512:NWARM + (grp + 1) * 512,
                                     k * 128:(k + 1) * 128],
                               r=[K("xnbuf", (NWARM // SUB) + grp * 4 + j)
                                  for j in range(4)],
                               w=[K("xnT", k)])
                p.dma("gpsimd", sXB[buf], t_xb[buf][:],
                      x_in[(NWARM + sub * 128):(NWARM + (sub + 1) * 128), :],
                      w=[K("t_xb", buf)])
                s4 = sub % 4
                for n in range(NSL2):
                    pb = (sub * 4 + n) % 8
                    for k in range(NK):
                        copq("tensor", lambda e, k=k, s4=s4, n=n, pb=pb:
                             e.matmul(
                                 out=pg[pb][:],
                                 lhsT=xnT[:, k * 512 + s4 * 128:
                                          k * 512 + (s4 + 1) * 128],
                                 rhs=wgset[:, k * DM + n * 512:
                                           k * DM + (n + 1) * 512],
                                 start=(k == 0), stop=False),
                             r=[K("xnT", k), K("wgset", k)],
                             w=([K("pg", pb)] if k == 0 else []))
                    cop("tensor", lambda e, n=n, pb=pb: e.matmul(
                        out=pg[pb][:], lhsT=c_ones_row[:],
                        rhs=c_bgb[0:1, n * 512:(n + 1) * 512],
                        start=False, stop=True),
                        r=[K("c", 1), K("c", 10)] +
                          [K("xnT", kk) for kk in range(NK)] +
                          [K("wgset", kk) for kk in range(NK)],
                        w=[K("pg", pb)])
                    cop("scalar", lambda e, n=n, pb=pb, buf=buf:
                        e.activation(
                        out=t_gate[buf][:, n * 512:(n + 1) * 512],
                        in_=pg[pb][:], func=AF.Sigmoid),
                        r=[K("pg", pb)], w=[K("gate", buf, n)])
                ln_b(t_xb[buf], t_sqb, clb[buf], K("t_xb", buf),
                     t_xn1[buf], K("t_xn1", buf), buf)
                cop("vector", lambda e, buf=buf: e.tensor_tensor(
                    out=t_yh[buf][:], in0=t_xn1[buf][:], in1=c_dp[:],
                    op=ALU.mult),
                    r=[K("t_xn1", buf), K("c", 9)], w=[K("t_yh", buf)])

                def yh_add(e, sub=sub, buf=buf):
                    y3 = y_all[:, sub * NH:(sub + 1) * NH].rearrange(
                        "p (h o) -> p h o", o=1)
                    t3 = t_yh[buf][:].rearrange("p (h d) -> p h d", d=64)
                    y3b, t3b = bass.broadcast_tensor_aps(y3, t3)
                    return e.tensor_tensor(out=t3, in0=t3b, in1=y3b, op=ALU.add)
                cop("vector", yh_add,
                    r=[K("t_yh", buf), K("yall", sub)], w=[K("t_yh", buf)])
                cop("vector", lambda e, buf=buf: e.tensor_tensor(
                    out=t_yfb[buf][:], in0=t_yh[buf][:], in1=t_gate[buf][:],
                    op=ALU.mult),
                    r=[K("t_yh", buf)] + [K("gate", buf, n)
                                          for n in range(NSL2)],
                    w=[K("t_yfb", buf)])
                p.dma("scalar", sYF[buf], yfbuf[sub * 128:(sub + 1) * 128, :],
                      t_yfb[buf][:], r=[K("t_yfb", buf)], w=[K("yfbuf", sub)])
            p.barrier()

        with ExitStack() as cb2:
            woset = sb("woset", (128, NK * DM), BF16, c=cb2)
            yfT = sb("yfT", (128, NK * 512), BF16, c=cb2)
            t_s = [sb(f"t_s{i}", (128, DM), c=cb2) for i in range(2)]
            t_gr = [sb(f"t_gr{i}", (128, DM), c=cb2) for i in range(2)]
            t_xb2 = [sb(f"t_xb2{i}", (128, DM), c=cb2) for i in range(2)]
            t_o = [sb(f"t_o{i}", (128, DM), c=cb2) for i in range(2)]
            po = [ps(f"po{i}", (128, 512), c=cb2) for i in range(8)]
            sWO = sem("sWO")
            sYT = sem("sYT")
            sGR = [sem("sGR0"), sem("sGR1")]
            sXB2 = [sem("sXB20"), sem("sXB21")]
            sOUT = [sem("sOUT0"), sem("sOUT1")]
            for k in range(NK):
                p.dma("gpsimd", sWO, woset[:, k * DM:(k + 1) * DM],
                      wo_in[k * 128:(k + 1) * 128, :], w=[K("woset", k)])
            for sub in range(NSUB):
                buf = sub % 2
                if sub % 4 == 0:
                    grp = sub // 4
                    for k in range(NK):
                        p.dmat("sync", sYT, yfT[:, k * 512:(k + 1) * 512],
                               yfbuf[grp * 512:(grp + 1) * 512,
                                     k * 128:(k + 1) * 128],
                               r=[K("yfbuf", grp * 4 + j) for j in range(4)],
                               w=[K("yfT", k)])
                p.dma("gpsimd", sGR[buf], t_gr[buf][:],
                      gbuf[sub * 128:(sub + 1) * 128, :],
                      r=[K("gbuf", sub, n) for n in range(NSL2)],
                      w=[K("t_gr", buf)])
                p.dma("gpsimd", sXB2[buf], t_xb2[buf][:],
                      x_in[(NWARM + sub * 128):(NWARM + (sub + 1) * 128), :],
                      w=[K("t_xb2", buf)])
                s4 = sub % 4
                for n in range(NSL2):
                    pb = (sub * 4 + n) % 8
                    for k in range(NK):
                        fn = (lambda e, k=k, s4=s4, n=n, pb=pb:
                              e.matmul(
                                  out=po[pb][:],
                                  lhsT=yfT[:, k * 512 + s4 * 128:
                                           k * 512 + (s4 + 1) * 128],
                                  rhs=woset[:, k * DM + n * 512:
                                            k * DM + (n + 1) * 512],
                                  start=(k == 0), stop=(k == NK - 1)))
                        if k < NK - 1:
                            copq("tensor", fn,
                                 r=[K("yfT", k), K("woset", k)],
                                 w=([K("po", pb)] if k == 0 else []))
                        else:
                            cop("tensor", fn,
                                r=[K("yfT", kk) for kk in range(NK)] +
                                  [K("woset", kk) for kk in range(NK)],
                                w=[K("po", pb)])
                    cop("scalar", lambda e, n=n, pb=pb, buf=buf:
                        e.activation(
                        out=t_s[buf][:, n * 512:(n + 1) * 512], in_=po[pb][:],
                        func=AF.Identity),
                        r=[K("po", pb)], w=[K("t_s", buf, n)])
                cop("vector", lambda e, buf=buf: e.tensor_tensor(
                    out=t_o[buf][:], in0=t_s[buf][:], in1=t_gr[buf][:],
                    op=ALU.mult),
                    r=[K("t_gr", buf)] + [K("t_s", buf, n)
                                          for n in range(NSL2)],
                    w=[K("t_o", buf)])
                cop("vector", lambda e, buf=buf: e.tensor_tensor(
                    out=t_o[buf][:], in0=t_o[buf][:], in1=t_xb2[buf][:],
                    op=ALU.add),
                    r=[K("t_o", buf), K("t_xb2", buf)], w=[K("t_o", buf)])
                p.dma("scalar", sOUT[buf], out_d[sub * 128:(sub + 1) * 128, :],
                      t_o[buf][:], r=[K("t_o", buf)], w=[K("outd", sub)])
            p.fence("gpsimd", [K("outd", sub) for sub in range(NSUB)])

    p.replay()
    return nc


def _shard_inputs(L0_attn_x, L0_attn_Wx, L0_attn_Wg, L0_attn_bg, L0_attn_Wo,
                  L0_attn_A, L0_attn_Dp, L0_ffn_W1, L0_ffn_W2):
    x = np.asarray(L0_attn_x, np.float32)
    wx = np.ascontiguousarray(np.asarray(L0_attn_Wx, np.float32))
    wg = np.ascontiguousarray(np.asarray(L0_attn_Wg, np.float32))
    wo = np.ascontiguousarray(np.asarray(L0_attn_Wo, np.float32))
    w1 = np.ascontiguousarray(np.asarray(L0_ffn_W1, np.float32))
    w2 = np.ascontiguousarray(np.asarray(L0_ffn_W2, np.float32))
    bg = np.ascontiguousarray(np.asarray(L0_attn_bg, np.float32).reshape(1, DM))
    at = np.ascontiguousarray(np.asarray(L0_attn_A, np.float32).T)
    dp = np.asarray(L0_attn_Dp, np.float32)
    dp_rep = np.ascontiguousarray(
        np.broadcast_to(np.repeat(dp, DM // NH)[None, :], (128, DM)))
    ones_row = np.ones((1, 128), ml_dtypes.bfloat16)
    ones_col = np.ones((128, 1), ml_dtypes.bfloat16)
    ones_row_f = np.ones((1, 128), np.float32)
    ones_col_f = np.ones((128, 1), np.float32)
    ident_b = np.eye(128, dtype=ml_dtypes.bfloat16)
    ident32_f = np.eye(32, dtype=np.float32)
    sel_h = np.zeros((NH, NH * 128), ml_dtypes.bfloat16)
    for h in range(NH):
        sel_h[h, h * 128:(h + 1) * 128] = 1.0

    in_maps = []
    spans = x.shape[1] // 4
    for k in range(8):
        bi, si = k // 4, k % 4
        lo = si * spans
        if si == 0:
            xpad = np.concatenate(
                [np.zeros((NWARM, DM), np.float32), x[bi, lo:lo + spans]],
                axis=0)
            woff = np.full((128, 1), -1e9, np.float32)
        else:
            xpad = np.ascontiguousarray(x[bi, lo - NWARM:lo + spans])
            woff = np.zeros((128, 1), np.float32)
        in_maps.append(dict(
            x=xpad, wx=wx, wg=wg, wo=wo, w1=w1, w2=w2, bg=bg, dp_rep=dp_rep,
            a_t=at, ones_row=ones_row, ones_col=ones_col,
            ones_row_f=ones_row_f, ones_col_f=ones_col_f, ident_b=ident_b,
            ident32_f=ident32_f, woff=woff, sel_h=sel_h))
    return in_maps


def kernel(**inputs):
    in_maps = _shard_inputs(**inputs)
    nc = bass.Bass("TRN2", target_bir_lowering=False)
    build_program(nc)
    res = run_bass_kernel_spmd(nc, in_maps, core_ids=list(range(8)))
    shards = [np.asarray(r["out"], np.float32) for r in res.results]
    out = np.empty((2, 16384, DM), np.float32)
    for k in range(8):
        bi, si = k // 4, k % 4
        out[bi, si * 4096:(si + 1) * 4096] = shards[k]
    return out


# version 23 (final)
# speedup vs baseline: 1.1194x; 1.0667x over previous
import numpy as np
import ml_dtypes
import concourse.bass as bass
import concourse.mybir as mybir
from concourse.bass_utils import run_bass_kernel_spmd
from contextlib import ExitStack

F32 = mybir.dt.float32
BF16 = mybir.dt.bfloat16
I8 = mybir.dt.int8
AF = mybir.ActivationFunctionType
ALU = mybir.AluOpType
AX = mybir.AxisListType

MAGIC = 12582912.0
NT = 4096
NWARM = 512
NTW = NT + NWARM
DM = 2048
DFF = 8192
NS = 128
NH = DM // 64
NSSM = 2 * NS + NH
SUB = 128
NSUB = NT // SUB
NSUBW = NTW // SUB
NSUP = 2
TSUP = NT // NSUP
LN_EPS = 1e-5

ENGINES = ("sync", "scalar", "vector", "tensor", "gpsimd")


class Sem:
    __slots__ = ("h", "n", "dma")

    def __init__(self, handle, dma=False):
        self.h = handle
        self.n = 0
        self.dma = dma

    def inc(self, k):
        self.n += k
        return self.n


class Prog:

    def __init__(self, nc):
        self.nc = nc
        self.st = {k: [] for k in ENGINES}
        self.res = {}
        self.water = {k: {} for k in ENGINES}
        self.sems = []

    def em(self, eng, fn):
        self.st[eng].append(fn)

    def raw_wait(self, eng, sem, n):
        if sem.dma:
            n = sem.n
        if n <= 0:
            return
        w = self.water[eng]
        if w.get(sem, 0) >= n:
            return
        w[sem] = n
        self.em(eng, lambda e, s=sem.h, n=n: e.wait_ge(s, n))

    def _need(self, eng, reads, writes):
        waits = []
        for k in reads:
            r = self.res.get(k)
            if r and r["w"]:
                waits.append(r["w"])
        for k in writes:
            r = self.res.get(k)
            if r:
                if r["w"]:
                    waits.append(r["w"])
                waits.extend(r["r"])
        for sem, n in waits:
            self.raw_wait(eng, sem, n)

    def _record(self, mark, reads, writes):
        for k in writes:
            self.res[k] = {"w": mark, "r": []}
        for k in reads:
            r = self.res.setdefault(k, {"w": None, "r": []})
            r["r"].append(mark)

    def dma(self, eng, sem, out, in_, r=(), w=(), **kw):
        sem.dma = True
        self._need(eng, r, w)
        n = sem.inc(16)
        self.em(eng, lambda e, o=out, i=in_, s=sem.h, kw=kw:
                e.dma_start(out=o, in_=i, **kw).then_inc(s, 16))
        self._record((sem, n), r, w)
        return n

    def dmat(self, eng, sem, out, in_, r=(), w=()):
        sem.dma = True
        self._need(eng, r, w)
        n = sem.inc(16)
        self.em(eng, lambda e, o=out, i=in_, s=sem.h:
                e.dma_start_transpose(out=o, in_=i).then_inc(s, 16))
        self._record((sem, n), r, w)
        return n

    def op(self, eng, sem, fn, r=(), w=()):
        self._need(eng, r, w)
        n = sem.inc(1)
        self.em(eng, lambda e, fn=fn, s=sem.h: fn(e).then_inc(s, 1))
        self._record((sem, n), r, w)
        return n

    def op_nomark(self, eng, fn, r=(), w=()):
        self._need(eng, r, w)
        self.em(eng, fn)

    def fence(self, eng, keys):
        waits = []
        for k in keys:
            r = self.res.get(k)
            if r:
                if r["w"]:
                    waits.append(r["w"])
                waits.extend(r["r"])
        for sem, n in waits:
            self.raw_wait(eng, sem, n)

    def barrier(self):
        for eng in ENGINES:
            for s in self.sems:
                self.raw_wait(eng, s, s.n)
        self.res.clear()

    def replay(self):
        with self.nc.Block() as block:
            for eng in ENGINES:
                deco = getattr(block, eng)
                stream = self.st[eng]

                def make(stream):
                    def body(e):
                        for fn in stream:
                            fn(e)
                    return body
                deco(make(stream))


def K(*a):
    return tuple(a)


def build_program(nc):
    NK = DM // SUB
    NKF = DFF // SUB
    NSL1 = DFF // 512
    NSL2 = DM // 512
    W1CH = DFF // SUB
    W2CH = DM // SUB
    NCHUNK = NTW // 512
    P4SLAB = NTW // 4
    NELW = DFF * DM

    def din(name, shape, dt=F32):
        return nc.dram_tensor(name, list(shape), dt, kind="ExternalInput").ap()

    def dscr(name, shape, dt):
        return nc.dram_tensor(name, list(shape), dt, kind="Internal").ap()

    x_in = din("x", (NTW, DM))
    wx_in = din("wx", (DM, NSSM))
    wg_in = din("wg", (DM, DM))
    wo_in = din("wo", (DM, DM))
    w1_in = din("w1", (DFF, DM))
    w2_in = din("w2", (DM, DFF))
    bg_in = din("bg", (1, DM))
    dp_in = din("dp_rep", (128, DM))
    at_in = din("a_t", (NS, NH))
    onesrow_in = din("ones_row", (1, 128), BF16)
    onescol_in = din("ones_col", (128, 1), BF16)
    onesrowf_in = din("ones_row_f", (1, 128))
    onescolf_in = din("ones_col_f", (128, 1))
    identb_in = din("ident_b", (128, 128), BF16)
    ident32f_in = din("ident32_f", (32, 32))
    woff_in = din("woff", (128, 1))
    sel_in = din("sel_h", (NH, NH * 128), BF16)

    out_d = nc.dram_tensor("out", [NT, DM], F32, kind="ExternalOutput").ap()

    xnbuf = dscr("xnbuf", (NTW, DM), BF16)
    q1buf = dscr("q1buf", (NT, DM), BF16)
    ssmbuf = dscr("ssmbuf", (NTW, NSSM), BF16)
    hbuf = dscr("hbuf", (TSUP, DFF), F32)
    gbuf = dscr("gbuf", (NT, DM), F32)
    yfbuf = dscr("yfbuf", (NT, DM), BF16)
    t1q = dscr("t1q", (DFF, DM), BF16)
    t2q = dscr("t2q", (DM, DFF), BF16)
    ybuf = dscr("ybuf", (NH, NT), F32)
    q2buf = dscr("q2buf", (TSUP, DFF), BF16)
    t1b = dscr("t1b", (NSL1, 128, NK, 512), I8)
    t2b = dscr("t2b", (NSL2, 128, NKF, 512), I8)

    p = Prog(nc)
    with ExitStack() as ctx:
        def sb(name, shape, dt=F32, c=ctx):
            return c.enter_context(nc.sbuf_tensor(name, list(shape), dt))

        def ps(name, shape, dt=F32, c=ctx):
            return c.enter_context(nc.psum_tensor(name, list(shape), dt))

        def sem(name):
            s = Sem(ctx.enter_context(nc.semaphore(name)))
            p.sems.append(s)
            return s

        eV, eT, eA, eG = sem("eV"), sem("eT"), sem("eA"), sem("eG")
        ESEM = {"vector": eV, "tensor": eT, "scalar": eA, "gpsimd": eG}

        def cop(eng, fn, r=(), w=()):
            return p.op(eng, ESEM[eng], fn, r=r, w=w)

        def copq(eng, fn, r=(), w=()):
            p.op_nomark(eng, fn, r=r, w=w)

        c_ones_row = sb("c_ones_row", (1, 128), BF16)
        c_ones_col = sb("c_ones_col", (128, 1), BF16)
        c_ones_rowf = sb("c_ones_rowf", (1, 128))
        c_ones_colf = sb("c_ones_colf", (128, 1))
        c_identb = sb("c_identb", (128, 128), BF16)
        c_ident32f = sb("c_ident32f", (32, 32))
        c_at = sb("c_at", (NS, NH))
        c_woff = sb("c_woff", (128, 1))
        c_bgb = sb("c_bgb", (1, DM), BF16)
        c_eps = sb("c_eps", (128, 1))
        c_negmag = sb("c_negmag", (128, 1))
        c_sel = sb("c_sel", (NH, NH * 128), BF16)
        c_mean12 = sb("c_mean12", (128, 2))
        c_ws12 = sb("c_ws12", (128, 2))
        evs1_all = sb("evs1_all", (128, NSUB))
        amax1_all = sb("amax1_all", (128, NSUB))
        amax2_all = sb("amax2_all", (128, TSUP // SUB))
        evs2_all = sb("evs2_all", (128, TSUP // SUB))
        y_all = sb("y_all", (128, NSUB * NH))

        sC, sCg = sem("sC"), sem("sCg")
        p.dma("sync", sC, c_ones_row[:], onesrow_in, w=[K("c", 1)])
        p.dma("sync", sC, c_ones_col[:], onescol_in, w=[K("c", 2)])
        p.dma("sync", sC, c_ones_rowf[:], onesrowf_in, w=[K("c", 3)])
        p.dma("sync", sC, c_ones_colf[:], onescolf_in, w=[K("c", 4)])
        p.dma("sync", sC, c_identb[:], identb_in, w=[K("c", 5)])
        p.dma("sync", sC, c_ident32f[:], ident32f_in, w=[K("c", 6)])
        p.dma("sync", sC, c_at[:], at_in, w=[K("c", 7)])
        p.dma("sync", sC, c_woff[:], woff_in, w=[K("c", 8)])
        p.dma("sync", sC, c_sel[:], sel_in, w=[K("c", 11)])
        p.dma("gpsimd", sCg, c_bgb[:], bg_in, w=[K("c", 10)])
        cop("vector", lambda e: e.memset(c_eps[:], LN_EPS), w=[K("c", "eps")])
        cop("vector", lambda e: e.memset(c_negmag[:], -MAGIC), w=[K("c", "nm")])
        with ExitStack() as c3:
            t_x = [sb(f"t_x{i}", (128, DM), c=c3) for i in range(2)]
            t_sq = sb("t_sq", (128, DM), c=c3)
            t_xn = [sb(f"t_xn{i}", (128, DM), c=c3) for i in range(2)]
            t_xnb = [sb(f"t_xnb{i}", (128, DM), BF16, c=c3) for i in range(2)]
            t_q1m = sb("t_q1m", (128, DM), c=c3)
            t_q1b = [sb(f"t_q1b{i}", (128, DM), BF16, c=c3) for i in range(2)]
            cl = [{nm: sb(f"cl{i}_" + nm, (128, 1), c=c3) for nm in
                   ("sum", "sq", "m", "m2", "v", "rstd", "negmr",
                    "amax", "am1c", "rec", "s1")} for i in range(2)]
            sX = [sem("sX0"), sem("sX1")]
            sXN = [sem("sXN0"), sem("sXN1")]
            sQ1 = [sem("sQ10"), sem("sQ11")]
            wb = [sb(f"wb{i}", (128, DFF), BF16, c=c3) for i in range(2)]
            wf = [sb(f"wf{i}", (128, DFF), c=c3) for i in range(2)]
            wacc = sb("wacc", (128, W1CH + W2CH), c=c3)
            rs2 = sb("rs2", (128, 2), c=c3)
            m12s = sb("m12s", (1, 2), c=c3)
            pm11 = ps("pm11", (1, 2), c=c3)
            pm128 = ps("pm128", (128, 2), c=c3)
            sWB = [sem("sWB0"), sem("sWB1")]
            sWF = [sem("sWF0"), sem("sWF1")]
            sWQ = [sem("sWQ0"), sem("sWQ1")]

            def ln_subtile(b, keyx):
                clb_ = cl[b]
                cop("vector", lambda e, b=b, clb_=clb_: e.reduce_sum(
                    out=clb_["sum"][:], in_=t_x[b][:], axis=AX.X),
                    r=[keyx], w=[K("cl", b, "sum")])
                cop("scalar", lambda e, b=b, clb_=clb_: e.activation(
                    out=t_sq[:], in_=t_x[b][:], func=AF.Square,
                    accum_out=clb_["sq"][:]),
                    r=[keyx], w=[K("t_sq"), K("cl", b, "sq")])
                cop("vector", lambda e, clb_=clb_: e.tensor_scalar(
                    out=clb_["m"][:], in0=clb_["sum"][:], scalar1=1.0 / DM,
                    scalar2=None, op0=ALU.mult),
                    r=[K("cl", b, "sum")], w=[K("cl", b, "m")])
                cop("vector", lambda e, clb_=clb_: e.tensor_tensor(
                    out=clb_["m2"][:], in0=clb_["m"][:], in1=clb_["m"][:],
                    op=ALU.mult), r=[K("cl", b, "m")], w=[K("cl", b, "m2")])
                cop("vector", lambda e, clb_=clb_: e.tensor_scalar(
                    out=clb_["v"][:], in0=clb_["sq"][:], scalar1=1.0 / DM,
                    scalar2=None, op0=ALU.mult),
                    r=[K("cl", b, "sq")], w=[K("cl", b, "v")])
                cop("vector", lambda e, clb_=clb_: e.tensor_tensor(
                    out=clb_["v"][:], in0=clb_["v"][:], in1=clb_["m2"][:],
                    op=ALU.subtract), r=[K("cl", b, "m2"), K("cl", b, "v")],
                    w=[K("cl", b, "v")])
                cop("scalar", lambda e, clb_=clb_: e.activation(
                    out=clb_["rstd"][:], in_=clb_["v"][:], func=AF.Ln,
                    bias=c_eps[:]),
                    r=[K("cl", b, "v"), K("c", "eps")], w=[K("cl", b, "rstd")])
                cop("scalar", lambda e, clb_=clb_: e.activation(
                    out=clb_["rstd"][:], in_=clb_["rstd"][:], func=AF.Exp,
                    scale=-0.5), r=[K("cl", b, "rstd")], w=[K("cl", b, "rstd")])
                cop("vector", lambda e, clb_=clb_: e.tensor_scalar(
                    out=clb_["negmr"][:], in0=clb_["m"][:],
                    scalar1=clb_["rstd"][:],
                    scalar2=-1.0, op0=ALU.mult, op1=ALU.mult),
                    r=[K("cl", b, "m"), K("cl", b, "rstd")],
                    w=[K("cl", b, "negmr")])
                cop("scalar", lambda e, b=b, clb_=clb_: e.activation(
                    out=t_xn[b][:], in_=t_x[b][:], func=AF.Identity,
                    scale=clb_["rstd"][:], bias=clb_["negmr"][:]),
                    r=[keyx, K("cl", b, "rstd"), K("cl", b, "negmr")],
                    w=[K("t_xn", b)])

            def p3_subtile(i):
                buf = i % 2
                real = i >= (NSUBW - NSUB)
                ri = i - (NSUBW - NSUB)
                p.dma("gpsimd", sX[buf], t_x[buf][:],
                      x_in[i * 128:(i + 1) * 128, :], w=[K("t_x", buf)])
                ln_subtile(buf, K("t_x", buf))
                cop("scalar", lambda e, b=buf: e.activation(
                    out=t_xnb[b][:], in_=t_x[b][:], func=AF.Identity,
                    scale=cl[b]["rstd"][:], bias=cl[b]["negmr"][:]),
                    r=[K("t_x", buf), K("cl", buf, "rstd"),
                       K("cl", buf, "negmr")],
                    w=[K("t_xnb", buf)])
                p.dma("scalar", sXN[buf], xnbuf[i * 128:(i + 1) * 128, :],
                      t_xnb[buf][:], r=[K("t_xnb", buf)], w=[K("xnbuf", i)])
                if real:
                    b_ = buf
                    clb_ = cl[b_]
                    cop("vector", lambda e, b_=b_, clb_=clb_: e.reduce_max(
                        out=clb_["amax"][:], in_=t_xn[b_][:], axis=AX.X,
                        apply_absolute_value=True),
                        r=[K("t_xn", b_)], w=[K("cl", b_, "amax")])
                    cop("vector", lambda e, b_=b_, clb_=clb_: e.tensor_scalar(
                        out=clb_["am1c"][:], in0=clb_["amax"][:], scalar1=1e-5,
                        scalar2=None, op0=ALU.max),
                        r=[K("cl", b_, "amax")], w=[K("cl", b_, "am1c")])
                    cop("vector", lambda e, r_=ri, clb_=clb_: e.tensor_copy(
                        out=amax1_all[:, r_:r_ + 1], in_=clb_["am1c"][:]),
                        r=[K("cl", b_, "am1c")], w=[K("amax1", ri)])
                    cop("vector", lambda e, clb_=clb_: e.reciprocal(
                        out=clb_["rec"][:], in_=clb_["am1c"][:]),
                        r=[K("cl", b_, "am1c")], w=[K("cl", b_, "rec")])
                    cop("vector", lambda e, clb_=clb_: e.tensor_scalar(
                        out=clb_["s1"][:], in0=clb_["rec"][:], scalar1=127.0,
                        scalar2=None, op0=ALU.mult),
                        r=[K("cl", b_, "rec")], w=[K("cl", b_, "s1")])
                    cop("vector", lambda e, b_=b_, clb_=clb_: e.tensor_scalar(
                        out=t_q1m[:], in0=t_xn[b_][:],
                        scalar1=clb_["s1"][:],
                        scalar2=MAGIC, op0=ALU.mult, op1=ALU.add),
                        r=[K("t_xn", b_), K("cl", b_, "s1")],
                        w=[K("t_q1m")])
                    cop("vector", lambda e, b_=b_: e.tensor_scalar(
                        out=t_q1m[:], in0=t_q1m[:], scalar1=MAGIC,
                        scalar2=127.0, op0=ALU.subtract, op1=ALU.min),
                        r=[K("t_q1m")], w=[K("t_q1m")])
                    cop("vector", lambda e, b=buf: e.tensor_scalar(
                        out=t_q1b[b][:], in0=t_q1m[:], scalar1=-128.0,
                        scalar2=None, op0=ALU.max),
                        r=[K("t_q1m")], w=[K("t_q1b", buf)])
                    p.dma("scalar", sQ1[buf],
                          q1buf[ri * 128:(ri + 1) * 128, :],
                          t_q1b[buf][:], r=[K("t_q1b", buf)],
                          w=[K("q1buf", ri)])

            def p1_mean_chunk(c):
                b = c % 2
                if c < W1CH:
                    src = w1_in[c * 128:(c + 1) * 128, :]
                    width = DM
                else:
                    src = w2_in[(c - W1CH) * 128:(c - W1CH + 1) * 128, :]
                    width = DFF
                p.dma("gpsimd", sWB[b], wb[b][:, 0:width], src, w=[K("wb", b)])
                cop("scalar", lambda e, b=b, c=c, width=width: e.activation(
                    out=wf[b][:, 0:width], in_=wb[b][:, 0:width], func=AF.Abs,
                    accum_out=wacc[:, c:c + 1]),
                    r=[K("wb", b)], w=[K("wf", b), K("wacc", c)])

            def p1_reduce():
                cop("vector", lambda e: e.reduce_sum(
                    out=rs2[:, 0:1], in_=wacc[:, 0:W1CH], axis=AX.X),
                    r=[K("wacc", c) for c in range(W1CH)], w=[K("rs2", 0)])
                cop("vector", lambda e: e.reduce_sum(
                    out=rs2[:, 1:2], in_=wacc[:, W1CH:W1CH + W2CH], axis=AX.X),
                    r=[K("wacc", W1CH + c) for c in range(W2CH)],
                    w=[K("rs2", 1)])
                cop("tensor", lambda e: e.matmul(
                    out=pm11[:], lhsT=c_ones_colf[:], rhs=rs2[:],
                    start=True, stop=True),
                    r=[K("rs2", 0), K("rs2", 1), K("c", 4)], w=[K("pm11")])
                cop("vector", lambda e: e.tensor_scalar(
                    out=m12s[:], in0=pm11[:], scalar1=1.0 / NELW, scalar2=1e-5,
                    op0=ALU.mult, op1=ALU.max),
                    r=[K("pm11")], w=[K("m12s")])
                cop("tensor", lambda e: e.matmul(
                    out=pm128[:], lhsT=c_ones_rowf[:], rhs=m12s[:],
                    start=True, stop=True),
                    r=[K("m12s"), K("c", 3)], w=[K("pm128")])
                cop("vector", lambda e: e.tensor_copy(
                    out=c_mean12[:], in_=pm128[:]),
                    r=[K("pm128")], w=[K("cm12")])
                cop("vector", lambda e: e.reciprocal(
                    out=c_ws12[:], in_=c_mean12[:]),
                    r=[K("cm12")], w=[K("cws12")])

            def p1_quant_chunk(c):
                b = c % 2
                if c < W1CH:
                    src = w1_in[c * 128:(c + 1) * 128, :]
                    dst = t1q[c * 128:(c + 1) * 128, :]
                    width = DM
                    wcol = 0
                else:
                    src = w2_in[(c - W1CH) * 128:(c - W1CH + 1) * 128, :]
                    dst = t2q[(c - W1CH) * 128:(c - W1CH + 1) * 128, :]
                    width = DFF
                    wcol = 1
                p.dma("sync", sWF[b], wf[b][:, 0:width], src, w=[K("wf", b)])
                cop("vector", lambda e, b=b, width=width, wcol=wcol:
                    e.tensor_scalar(
                        out=wf[b][:, 0:width], in0=wf[b][:, 0:width],
                        scalar1=c_ws12[:, wcol:wcol + 1], scalar2=MAGIC,
                        op0=ALU.mult, op1=ALU.add),
                    r=[K("wf", b), K("cws12")], w=[K("wf", b)])
                cop("scalar", lambda e, b=b, width=width: e.activation(
                    out=wf[b][:, 0:width], in_=wf[b][:, 0:width],
                    func=AF.Identity, bias=c_negmag[:]),
                    r=[K("wf", b), K("c", "nm")], w=[K("wf", b)])
                cop("vector", lambda e, b=b, width=width: e.tensor_scalar(
                    out=wb[b][:, 0:width], in0=wf[b][:, 0:width],
                    scalar1=-1.0, scalar2=1.0, op0=ALU.max, op1=ALU.min),
                    r=[K("wf", b)], w=[K("wb", b)])
                p.dma("scalar", sWQ[b], dst, wb[b][:, 0:width],
                      r=[K("wb", b)], w=[K("tq", c)])

            NCH = W1CH + W2CH
            mc = 0
            for i in range(NSUBW):
                p3_subtile(i)
                want = (i + 1) * NCH // NSUBW
                while mc < want:
                    p1_mean_chunk(mc)
                    mc += 1
            while mc < NCH:
                p1_mean_chunk(mc)
                mc += 1
            p1_reduce()
            cop("vector", lambda e: e.tensor_scalar(
                out=evs1_all[:], in0=amax1_all[:], scalar1=c_mean12[:, 0:1],
                scalar2=1.0 / 127.0, op0=ALU.mult, op1=ALU.mult),
                r=[K("cm12")] + [K("amax1", r_) for r_ in range(NSUB)],
                w=[K("evs1")])
            for c in range(NCH):
                p1_quant_chunk(c)
            p.barrier()

        with ExitStack() as c4:
            c_wxb = sb("c_wxb", (128, NK * NSSM), BF16, c=c4)
            db_T = sb("db_T", (NH, NTW), BF16, c=c4)
            for k in range(NK):
                p.dma("gpsimd", sCg, c_wxb[:, k * NSSM:(k + 1) * NSSM],
                      wx_in[k * 128:(k + 1) * 128, :], w=[K("c", "wx", k)])
            sxnt = sb("sxnt", (128, NK * P4SLAB), BF16, c=c4)
            t_ssm = [sb(f"t_ssm{i}", (128, NSSM), BF16, c=c4) for i in range(2)]
            pwx = [ps(f"pwx{i}", (128, NSSM), c=c4) for i in range(2)]
            pdb = ps("pdb", (NH, 128), BF16, c=c4)
            sT4 = sem("sT4")
            sM4 = sem("sM4")
            nsub_slab = P4SLAB // SUB
            for slab in range(4):
                for k in range(NK):
                    p.dmat("sync", sT4, sxnt[:, k * P4SLAB:(k + 1) * P4SLAB],
                           xnbuf[slab * P4SLAB:(slab + 1) * P4SLAB,
                                 k * 128:(k + 1) * 128],
                           r=[K("xnbuf", slab * nsub_slab + j)
                              for j in range(nsub_slab)],
                           w=[K("sxnt", k)])
                for j in range(nsub_slab):
                    i = slab * nsub_slab + j
                    pb = i % 2
                    for k in range(NK):
                        fn = (lambda e, k=k, j=j, pb=pb: e.matmul(
                            out=pwx[pb][:],
                            lhsT=sxnt[:, k * P4SLAB + j * 128:
                                      k * P4SLAB + (j + 1) * 128],
                            rhs=c_wxb[:, k * NSSM:(k + 1) * NSSM],
                            start=(k == 0), stop=(k == NK - 1)))
                        if k < NK - 1:
                            copq("tensor", fn,
                                 r=[K("sxnt", k), K("c", "wx", k)],
                                 w=([K("pwx", pb)] if k == 0 else []))
                        else:
                            cop("tensor", fn,
                                r=[K("sxnt", kk) for kk in range(NK)] +
                                  [K("c", "wx", kk) for kk in range(NK)],
                                w=[K("pwx", pb)])
                    if i < NWARM // SUB:
                        cop("vector", lambda e, pb=pb: e.tensor_scalar(
                            out=t_ssm[pb][:], in0=pwx[pb][:],
                            scalar1=c_woff[:], scalar2=None, op0=ALU.add),
                            r=[K("pwx", pb), K("c", 8)], w=[K("t_ssm", pb)])
                    else:
                        cop("scalar", lambda e, pb=pb: e.activation(
                            out=t_ssm[pb][:], in_=pwx[pb][:], func=AF.Identity),
                            r=[K("pwx", pb)], w=[K("t_ssm", pb)])
                    p.dma("scalar", sM4, ssmbuf[i * 128:(i + 1) * 128, :],
                          t_ssm[pb][:], r=[K("t_ssm", pb)], w=[K("ssm", i)])
                    cop("tensor", lambda e, pb=pb: e.transpose(
                        out=pdb[:], in_=t_ssm[pb][:, 2 * NS:2 * NS + NH],
                        identity=c_identb[:]),
                        r=[K("t_ssm", pb), K("c", 5)], w=[K("pdb")])
                    cop("scalar", lambda e, i=i: e.activation(
                        out=db_T[:, i * 128:(i + 1) * 128], in_=pdb[:],
                        func=AF.Identity),
                        r=[K("pdb")], w=[K("dbT", i)])

            bB = [sb(f"bB{i}", (128, 512), BF16, c=c4) for i in range(2)]
            fB = [sb(f"fB{i}", (128, 512), c=c4) for i in range(2)]
            bC = [sb(f"bC{i}", (128, 512), BF16, c=c4) for i in range(2)]
            fC = [sb(f"fC{i}", (128, 512), c=c4) for i in range(2)]
            t_zt = [sb(f"t_zt{i}", (128, 512), c=c4) for i in range(2)]
            t_e = [sb(f"t_e{i}", (128, 512), c=c4) for i in range(2)]
            t_d5 = [sb(f"t_d5{i}", (128, 512), c=c4) for i in range(2)]
            t_a5 = [sb(f"t_a5{i}", (128, 512), c=c4) for i in range(2)]
            t_u5 = [sb(f"t_u5{i}", (128, 512), c=c4) for i in range(2)]
            t_hs = [sb(f"t_hs{i}", (128, 512), c=c4) for i in range(2)]
            t_mb = [sb(f"t_mb{i}", (128, 512), BF16, c=c4) for i in range(2)]
            st8 = sb("st8", (128, NH), c=c4)
            t_yrow = [sb(f"t_yrow{i}", (1, 512), c=c4) for i in range(2)]
            t_yb32 = sb("t_yb32", (NH, 128), c=c4)
            pz = [ps(f"pz{i}", (128, 512), c=c4) for i in range(2)]
            py = [ps(f"py{i}", (1, 512), c=c4) for i in range(2)]
            pytr = ps("pytr", (128, NH), c=c4)
            sB5 = [sem("sB5a"), sem("sB5b")]
            sC5 = [sem("sC5a"), sem("sC5b")]
            sYR = [sem("sYR0"), sem("sYR1")]
            sYL = sem("sYL")
            tt = [sb(f"tt{i}", (128, DFF), BF16, c=c4) for i in range(2)]
            sTT = [sem("sTT0"), sem("sTT1")]
            sTB = [sem("sTB0"), sem("sTB1")]
            tt2 = [sb(f"tt2_{i}", (128, DM), BF16, c=c4) for i in range(2)]
            sT2T = [sem("sT2T0"), sem("sT2T1")]
            sT2B = [sem("sT2B0"), sem("sT2B1")]

            def p2t1_unit(d):
                b = d % 2
                p.dmat("sync", sTT[b], tt[b][:, 0:DFF],
                       t1q[0:DFF, d * 128:(d + 1) * 128],
                       r=[K("tq", c) for c in range(W1CH)], w=[K("tt", b)])
                for n in range(NSL1):
                    p.dma("gpsimd", sTB[b], t1b[n, :, d, :],
                          tt[b][:, n * 512:(n + 1) * 512],
                          r=[K("tt", b)], w=[K("t1b", n, d)])

            def p2t2_unit(f):
                b2 = f % 2
                p.dmat("sync", sT2T[b2], tt2[b2][:, 0:DM],
                       t2q[0:DM, f * 128:(f + 1) * 128],
                       w=[K("tt2", b2)])
                for n in range(NSL2):
                    p.dma("gpsimd", sT2B[b2], t2b[n, :, f, :],
                          tt2[b2][:, n * 512:(n + 1) * 512],
                          r=[K("tt2", b2)], w=[K("t2b", n, f)])

            p2_thunks = ([(lambda d=d: p2t1_unit(d)) for d in range(NK)] +
                         [(lambda f=f: p2t2_unit(f)) for f in range(NKF)])

            def issue_chunk_loads(tc):
                cb = tc % 2
                p.dmat("sync", sB5[cb], bB[cb][:],
                       ssmbuf[tc * 512:(tc + 1) * 512, 0:NS],
                       r=[K("ssm", tc * 4 + j) for j in range(4)],
                       w=[K("bB", cb)])
                if tc >= NWARM // 512:
                    p.dmat("sync", sC5[cb], bC[cb][:],
                           ssmbuf[tc * 512:(tc + 1) * 512, NS:2 * NS],
                           r=[K("ssm", tc * 4 + j) for j in range(4)],
                           w=[K("bC", cb)])

            issue_chunk_loads(0)
            for tc in range(NCHUNK):
                cb = tc % 2
                realc = tc >= NWARM // 512
                rc = tc - NWARM // 512
                if tc + 1 < NCHUNK:
                    issue_chunk_loads(tc + 1)
                cop("vector", lambda e, cb=cb: e.tensor_copy(
                    out=fB[cb][:], in_=bB[cb][:]),
                    r=[K("bB", cb)], w=[K("fB", cb)])
                if realc:
                    cop("vector", lambda e, cb=cb: e.tensor_copy(
                        out=fC[cb][:], in_=bC[cb][:]),
                        r=[K("bC", cb)], w=[K("fC", cb)])
                def scan_head(h, tc=tc, cb=cb):
                    pb = h % 2
                    cop("tensor", lambda e, h=h, tc=tc, pb=pb: e.matmul(
                        out=pz[pb][:], lhsT=c_sel[:, h * 128:(h + 1) * 128],
                        rhs=db_T[:, tc * 512:(tc + 1) * 512],
                        start=True, stop=True),
                        r=[K("dbT", tc * 4), K("dbT", tc * 4 + 1),
                           K("dbT", tc * 4 + 2), K("dbT", tc * 4 + 3),
                           K("c", 11)],
                        w=[K("pz", pb)])
                    cop("vector", lambda e, pb=pb, cb=cb: e.tensor_tensor(
                        out=t_zt[pb][:], in0=pz[pb][:], in1=fB[cb][:],
                        op=ALU.add),
                        r=[K("pz", pb), K("fB", cb)], w=[K("zt", pb)])
                    cop("scalar", lambda e, pb=pb: e.activation(
                        out=t_e[pb][:], in_=t_zt[pb][:], func=AF.Exp),
                        r=[K("zt", pb)], w=[K("te", pb)])
                    cop("scalar", lambda e, pb=pb: e.activation(
                        out=t_d5[pb][:], in_=t_e[pb][:], func=AF.Ln, bias=1.0),
                        r=[K("te", pb)], w=[K("td", pb)])
                    cop("scalar", lambda e, h=h, pb=pb: e.activation(
                        out=t_a5[pb][:], in_=t_d5[pb][:], func=AF.Exp,
                        scale=c_at[:, h:h + 1]),
                        r=[K("td", pb), K("c", 7)], w=[K("ta", pb)])
                    cop("scalar", lambda e, pb=pb: e.activation(
                        out=t_u5[pb][:], in_=t_d5[pb][:], func=AF.Square),
                        r=[K("td", pb)], w=[K("tu", pb)])

                def scan_tail(h, tc=tc, cb=cb, rc=rc, realc=realc):
                    pb = h % 2
                    if tc == 0:
                        cop("vector", lambda e, pb=pb: e.tensor_tensor_scan(
                            out=t_hs[pb][:], data0=t_a5[pb][:],
                            data1=t_u5[pb][:],
                            initial=0.0, op0=ALU.mult, op1=ALU.add),
                            r=[K("ta", pb), K("tu", pb)], w=[K("hs", pb)])
                    else:
                        cop("vector", lambda e, h=h, pb=pb:
                            e.tensor_tensor_scan(
                                out=t_hs[pb][:], data0=t_a5[pb][:],
                                data1=t_u5[pb][:],
                                initial=st8[:, h:h + 1], op0=ALU.mult,
                                op1=ALU.add),
                            r=[K("ta", pb), K("tu", pb), K("st8", h)],
                            w=[K("hs", pb)])
                    cop("vector", lambda e, h=h, pb=pb: e.tensor_copy(
                        out=st8[:, h:h + 1], in_=t_hs[pb][:, 511:512]),
                        r=[K("hs", pb)], w=[K("st8", h)])
                    if realc:
                        cop("vector", lambda e, pb=pb, cb=cb: e.tensor_tensor(
                            out=t_mb[pb][:], in0=t_hs[pb][:], in1=fC[cb][:],
                            op=ALU.mult),
                            r=[K("hs", pb), K("fC", cb)], w=[K("mb", pb)])
                        cop("tensor", lambda e, pb=pb: e.matmul(
                            out=py[pb][:], lhsT=c_ones_col[:],
                            rhs=t_mb[pb][:],
                            start=True, stop=True),
                            r=[K("mb", pb), K("c", 2)], w=[K("py", pb)])
                        yb = h % 2
                        cop("vector", lambda e, pb=pb, yb=yb: e.tensor_copy(
                            out=t_yrow[yb][:], in_=py[pb][:]),
                            r=[K("py", pb)], w=[K("t_yrow", yb)])
                        p.dma("gpsimd", sYR[yb],
                              ybuf[h:h + 1, rc * 512:(rc + 1) * 512],
                              t_yrow[yb][:], r=[K("t_yrow", yb)],
                              w=[K("ybuf", rc, h)])

                for h in range(NH):
                    scan_head(h)
                    if h >= 1:
                        scan_tail(h - 1)
                    if p2_thunks and h % 3 == 1:
                        p2_thunks.pop(0)()
                scan_tail(NH - 1)
            while p2_thunks:
                p2_thunks.pop(0)()
            for s_ in range(NSUB):
                p.dma("gpsimd", sYL, t_yb32[:],
                      ybuf[:, s_ * 128:(s_ + 1) * 128],
                      r=[K("ybuf", s_ // 4, h) for h in range(NH)],
                      w=[K("t_yb32")])
                cop("tensor", lambda e: e.transpose(
                    out=pytr[:], in_=t_yb32[:],
                    identity=c_ident32f[0:NH, 0:NH]),
                    r=[K("t_yb32"), K("c", 6)], w=[K("pytr")])
                cop("vector", lambda e, s_=s_: e.tensor_copy(
                    out=y_all[:, s_ * NH:(s_ + 1) * NH], in_=pytr[:]),
                    r=[K("pytr")], w=[K("yall", s_)])
            p.barrier()
        for s_ in range(NSUP):
            nsub_s = TSUP // SUB
            with ExitStack() as c6:
                q1T = sb(f"q1T{s_}", (128, NK * TSUP), BF16, c=c6)
                rhs1 = [sb(f"rhs1_{s_}_{i}", (128, NK * 512), BF16, c=c6)
                        for i in range(2)]
                t_h = [sb(f"t_h{s_}_{i}", (128, 512), c=c6) for i in range(4)]
                cpam = sb(f"cpam{s_}", (128, 1), c=c6)
                pw1 = [ps(f"pw1_{s_}_{i}", (128, 512), c=c6) for i in range(4)]
                sQT = sem(f"sQT{s_}")
                sR1 = [sem(f"sR1a{s_}"), sem(f"sR1b{s_}")]
                sH = [sem(f"sH{s_}_{i}") for i in range(4)]
                for k in range(NK):
                    p.dmat("sync", sQT, q1T[:, k * TSUP:(k + 1) * TSUP],
                           q1buf[s_ * TSUP:(s_ + 1) * TSUP,
                                 k * 128:(k + 1) * 128],
                           r=[K("q1buf", s_ * nsub_s + j)
                              for j in range(nsub_s)],
                           w=[K("q1T", k)])
                for n in range(NSL1):
                    rb = n % 2
                    p.dma("gpsimd", sR1[rb], rhs1[rb][:],
                          t1b[n].rearrange("p k f -> p (k f)"),
                          r=[K("t1b", n, d) for d in range(NK)],
                          w=[K("rhs1", rb)])
                    for j in range(nsub_s):
                        pb = j % 4
                        for k in range(NK):
                            fn = (lambda e, k=k, j=j, pb=pb, rb=rb,
                                  pw1=pw1, q1T=q1T, rhs1=rhs1:
                                  e.matmul(
                                      out=pw1[pb][:],
                                      lhsT=q1T[:, k * TSUP + j * 128:
                                               k * TSUP + (j + 1) * 128],
                                      rhs=rhs1[rb][:, k * 512:(k + 1) * 512],
                                      start=(k == 0), stop=(k == NK - 1)))
                            if k < NK - 1:
                                copq("tensor", fn,
                                     r=[K("q1T", k), K("rhs1", rb)],
                                     w=([K("pw1", pb)] if k == 0 else []))
                            else:
                                cop("tensor", fn,
                                    r=[K("q1T", kk) for kk in range(NK)] +
                                      [K("rhs1", rb)],
                                    w=[K("pw1", pb)])
                        cop("scalar", lambda e, pb=pb, j=j, s_=s_,
                            t_h=t_h, pw1=pw1: e.activation(
                            out=t_h[pb][:], in_=pw1[pb][:], func=AF.Silu,
                            scale=evs1_all[:, s_ * nsub_s + j:
                                           s_ * nsub_s + j + 1]),
                            r=[K("pw1", pb), K("evs1")], w=[K("t_h", pb)])
                        cop("vector", lambda e, pb=pb, cpam=cpam, t_h=t_h:
                            e.reduce_max(
                            out=cpam[:], in_=t_h[pb][:], axis=AX.X,
                            apply_absolute_value=True),
                            r=[K("t_h", pb)], w=[K("cpam")])
                        if n == 0:
                            cop("vector", lambda e, j=j, cpam=cpam: e.tensor_copy(
                                out=amax2_all[:, j:j + 1], in_=cpam[:]),
                                r=[K("cpam")], w=[K("amax2", j)])
                        else:
                            cop("vector", lambda e, j=j, cpam=cpam: e.tensor_tensor(
                                out=amax2_all[:, j:j + 1],
                                in0=amax2_all[:, j:j + 1], in1=cpam[:],
                                op=ALU.max),
                                r=[K("cpam"), K("amax2", j)],
                                w=[K("amax2", j)])
                        p.dma("scalar", sH[pb],
                              hbuf[j * 128:(j + 1) * 128,
                                   n * 512:(n + 1) * 512],
                              t_h[pb][:], r=[K("t_h", pb)],
                              w=[K("hbuf", j, n)])
                p.barrier()
            with ExitStack() as c7:
                GT = 256
                NGRP = TSUP // GT
                NQ = NKF // 8
                q2Tg = [sb(f"q2Tg{s_}_{i}", (128, NKF * GT), BF16, c=c7)
                        for i in range(2)]
                rhs2 = [sb(f"rhs2_{s_}_{i}", (128, NQ * 512), BF16, c=c7)
                        for i in range(2)]
                t_hr = sb(f"t_hr{s_}", (128, DFF // 2), c=c7)
                t_q2b = [sb(f"t_q2b{s_}_{i}", (128, DFF), BF16, c=c7)
                         for i in range(4)]
                t_g = [sb(f"t_g{s_}_{i}", (128, 512), c=c7) for i in range(2)]
                cq = {nm: sb(f"cq{s_}_" + nm, (128, 1), c=c7) for nm in
                      ("am2c", "rec", "s2", "mag")}
                pq2 = [ps(f"pq2_{s_}_{i}", (128, 512), BF16, c=c7)
                       for i in range(2)]
                pw2 = [ps(f"pw2_{s_}_{i}", (128, 512), c=c7) for i in range(2)]
                sHR = sem(f"sHR{s_}")
                sR2 = [sem(f"sR2a{s_}"), sem(f"sR2b{s_}")]
                sG = [sem(f"sGa{s_}"), sem(f"sGb{s_}")]
                HH = DFF // 2
                cop("vector", lambda e, cq=cq: e.memset(cq["mag"][:], MAGIC),
                    w=[K("cq", "mag")])

                def p7_quant(j):
                    jb = j % 4
                    cop("vector", lambda e, j=j, cq=cq: e.tensor_scalar(
                        out=cq["am2c"][:], in0=amax2_all[:, j:j + 1],
                        scalar1=1e-5, scalar2=None, op0=ALU.max),
                        r=[K("amax2", j)], w=[K("cq", "am2c")])
                    cop("vector", lambda e, cq=cq: e.reciprocal(
                        out=cq["rec"][:], in_=cq["am2c"][:]),
                        r=[K("cq", "am2c")], w=[K("cq", "rec")])
                    cop("vector", lambda e, cq=cq: e.tensor_scalar(
                        out=cq["s2"][:], in0=cq["rec"][:], scalar1=127.0,
                        scalar2=None, op0=ALU.mult),
                        r=[K("cq", "rec")], w=[K("cq", "s2")])
                    cop("vector", lambda e, j=j, cq=cq: e.tensor_scalar(
                        out=evs2_all[:, j:j + 1], in0=cq["am2c"][:],
                        scalar1=c_mean12[:, 1:2], scalar2=1.0 / 127.0,
                        op0=ALU.mult, op1=ALU.mult),
                        r=[K("cq", "am2c"), K("cm12")], w=[K("evs2", j)])
                    for hh in range(2):
                        p.dma("scalar", sHR, t_hr[:],
                              hbuf[j * 128:(j + 1) * 128,
                                   hh * HH:(hh + 1) * HH],
                              r=[K("hbuf", j, n) for n in range(NSL1)],
                              w=[K("t_hr")])
                        cop("scalar", lambda e, t_hr=t_hr, cq=cq:
                            e.activation(
                                out=t_hr[:], in_=t_hr[:],
                                func=AF.Identity, scale=cq["s2"][:],
                                bias=cq["mag"][:]),
                            r=[K("t_hr"), K("cq", "s2"), K("cq", "mag")],
                            w=[K("t_hr")])
                        cop("vector", lambda e, t_hr=t_hr: e.tensor_scalar(
                            out=t_hr[:], in0=t_hr[:],
                            scalar1=MAGIC, scalar2=127.0,
                            op0=ALU.subtract, op1=ALU.min),
                            r=[K("t_hr")], w=[K("t_hr")])
                        cop("vector", lambda e, hh=hh, jb=jb,
                            t_q2b=t_q2b, t_hr=t_hr: e.tensor_scalar(
                                out=t_q2b[jb][:, hh * HH:(hh + 1) * HH],
                                in0=t_hr[:],
                                scalar1=-128.0, scalar2=None, op0=ALU.max),
                            r=[K("t_hr")], w=[K("t_q2b", jb, hh)])

                def p7_grp_tr(grp):
                    gb = grp % 2
                    for s2_ in range(2):
                        j = grp * 2 + s2_
                        jb = j % 4
                        for kg in range(NKF // 4):
                            pb = kg % 2
                            for kk in range(4):
                                k = kg * 4 + kk
                                fn = (lambda e, k=k, kk=kk, pb=pb, jb=jb,
                                      pq2=pq2, t_q2b=t_q2b: e.transpose(
                                          out=pq2[pb][:, kk * 128:
                                                      (kk + 1) * 128],
                                          in_=t_q2b[jb][:, k * 128:
                                                        (k + 1) * 128],
                                          identity=c_identb[:]))
                                if kk == 0:
                                    copq("tensor", fn,
                                         r=[K("t_q2b", jb, k // (NKF // 2)),
                                            K("c", 5)],
                                         w=[K("pq2", pb)])
                                elif kk < 3:
                                    copq("tensor", fn,
                                         r=[K("t_q2b", jb,
                                              k // (NKF // 2)), K("c", 5)])
                                else:
                                    cop("tensor", fn,
                                        r=[K("t_q2b", jb, 0),
                                           K("t_q2b", jb, 1), K("c", 5)],
                                        w=[K("pq2", pb)])
                            cop("vector", lambda e, kg=kg, s2_=s2_, gb=gb,
                                pb=pb, q2Tg=q2Tg, pq2=pq2:
                                e.tensor_copy(
                                    out=q2Tg[gb][:].rearrange(
                                        "p (k t) -> p k t", t=GT)[
                                        :, 4 * kg:4 * kg + 4,
                                        s2_ * 128:(s2_ + 1) * 128],
                                    in_=pq2[pb][:].rearrange(
                                        "p (k t) -> p k t", t=128)),
                                r=[K("pq2", pb)], w=[K("q2Tg", gb, kg)])

                def p7_grp_mms(grp):
                    gb = grp % 2
                    for n in range(NSL2):
                        for q in range(8):
                            rb = q % 2
                            p.dma("gpsimd", sR2[rb], rhs2[rb][:],
                                  t2b[n, :, q * NQ:(q + 1) * NQ, :]
                                  .rearrange("p k f -> p (k f)"),
                                  r=[K("t2b", n, f) for f in
                                     range(q * NQ, (q + 1) * NQ)],
                                  w=[K("rhs2", rb)])
                            for s2_ in range(2):
                                for kk in range(NQ):
                                    k = q * NQ + kk
                                    fn = (lambda e, k=k, kk=kk, rb=rb, s2_=s2_,
                                          gb=gb, pw2=pw2, q2Tg=q2Tg, rhs2=rhs2:
                                          e.matmul(
                                              out=pw2[s2_][:],
                                              lhsT=q2Tg[gb][:,
                                                           k * GT + s2_ * 128:
                                                           k * GT +
                                                           (s2_ + 1) * 128],
                                              rhs=rhs2[rb][:, kk * 512:
                                                           (kk + 1) * 512],
                                              start=(k == 0),
                                              stop=(k == NKF - 1)))
                                    if k == NKF - 1:
                                        cop("tensor", fn,
                                            r=[K("q2Tg", gb, kk2)
                                               for kk2 in range(NKF // 4)] +
                                              [K("rhs2", 0), K("rhs2", 1)],
                                            w=[K("pw2", s2_)])
                                    elif k == 0:
                                        cop("tensor", fn,
                                            r=[K("q2Tg", gb, 0),
                                               K("rhs2", rb)],
                                            w=[K("pw2", s2_)])
                                    elif kk == NQ - 1:
                                        cop("tensor", fn,
                                            r=[K("q2Tg", gb, k // 4),
                                               K("rhs2", rb)], w=[])
                                    else:
                                        copq("tensor", fn,
                                             r=[K("q2Tg", gb, k // 4),
                                                K("rhs2", rb)])
                        for s2_ in range(2):
                            j = grp * 2 + s2_
                            cop("scalar", lambda e, s2_=s2_, j=j,
                                t_g=t_g, pw2=pw2: e.activation(
                                out=t_g[s2_][:], in_=pw2[s2_][:],
                                func=AF.Silu,
                                scale=evs2_all[:, j:j + 1]),
                                r=[K("pw2", s2_), K("evs2", j)],
                                w=[K("t_g", s2_)])
                            p.dma("scalar", sG[s2_],
                                  gbuf[(s_ * TSUP + j * 128):
                                       (s_ * TSUP + (j + 1) * 128),
                                       n * 512:(n + 1) * 512],
                                  t_g[s2_][:], r=[K("t_g", s2_)],
                                  w=[K("gbuf", s_ * (TSUP // SUB) + j, n)])

                for j in range(min(4, TSUP // SUB)):
                    p7_quant(j)
                p7_grp_tr(0)
                for grp in range(NGRP):
                    if grp + 1 < NGRP:
                        p7_grp_tr(grp + 1)
                    for j in range(2 * (grp + 2), 2 * (grp + 3)):
                        if j < TSUP // SUB:
                            p7_quant(j)
                    p7_grp_mms(grp)
                p.barrier()

        with ExitStack() as cb1:
            c_dp = sb("c_dp", (128, DM), c=cb1)
            p.dma("sync", sC, c_dp[:], dp_in, w=[K("c", 9)])
            wgset = sb("wgset", (128, NK * DM), BF16, c=cb1)
            xnT = [sb(f"xnT{i}", (128, NK * 512), BF16, c=cb1)
                   for i in range(2)]
            t_xb = [sb(f"t_xb{i}", (128, DM), c=cb1) for i in range(2)]
            t_sqb = sb("t_sqb", (128, DM), c=cb1)
            t_xn1 = [sb(f"t_xn1{i}", (128, DM), c=cb1) for i in range(2)]
            t_gate = [sb(f"t_gate{i}", (128, DM), c=cb1) for i in range(2)]
            t_yh = [sb(f"t_yh{i}", (128, DM), c=cb1) for i in range(2)]
            t_yfb = [sb(f"t_yfb{i}", (128, DM), BF16, c=cb1) for i in range(2)]
            clb = [{nm: sb(f"clb{i}_" + nm, (128, 1), c=cb1) for nm in
                    ("sum", "sq", "m", "m2", "v", "rstd", "negmr")}
                   for i in range(2)]
            pg = [ps(f"pg{i}", (128, 512), c=cb1) for i in range(8)]
            sWG = sem("sWG")
            sXB = [sem("sXB0"), sem("sXB1")]
            sXT = [sem("sXT0"), sem("sXT1")]
            sYF = [sem("sYF0"), sem("sYF1")]
            for k in range(NK):
                p.dma("gpsimd", sWG, wgset[:, k * DM:(k + 1) * DM],
                      wg_in[k * 128:(k + 1) * 128, :], w=[K("wgset", k)])

            def ln_b(t_xt, t_sqt, clx, keyx, t_out, keyout, kb):
                cop("vector", lambda e, t_xt=t_xt, clx=clx: e.reduce_sum(
                    out=clx["sum"][:], in_=t_xt[:], axis=AX.X),
                    r=[keyx], w=[K("clb", kb, "sum")])
                cop("scalar", lambda e, t_xt=t_xt, t_sqt=t_sqt, clx=clx:
                    e.activation(
                    out=t_sqt[:], in_=t_xt[:], func=AF.Square,
                    accum_out=clx["sq"][:]),
                    r=[keyx], w=[K("tsqb"), K("clb", kb, "sq")])
                cop("vector", lambda e, clx=clx: e.tensor_scalar(
                    out=clx["m"][:], in0=clx["sum"][:], scalar1=1.0 / DM,
                    scalar2=None, op0=ALU.mult),
                    r=[K("clb", kb, "sum")], w=[K("clb", kb, "m")])
                cop("vector", lambda e, clx=clx: e.tensor_tensor(
                    out=clx["m2"][:], in0=clx["m"][:], in1=clx["m"][:],
                    op=ALU.mult), r=[K("clb", kb, "m")], w=[K("clb", kb, "m2")])
                cop("vector", lambda e, clx=clx: e.tensor_scalar(
                    out=clx["v"][:], in0=clx["sq"][:], scalar1=1.0 / DM,
                    scalar2=None, op0=ALU.mult),
                    r=[K("clb", kb, "sq")], w=[K("clb", kb, "v")])
                cop("vector", lambda e, clx=clx: e.tensor_tensor(
                    out=clx["v"][:], in0=clx["v"][:], in1=clx["m2"][:],
                    op=ALU.subtract), r=[K("clb", kb, "m2"), K("clb", kb, "v")],
                    w=[K("clb", kb, "v")])
                cop("scalar", lambda e, clx=clx: e.activation(
                    out=clx["rstd"][:], in_=clx["v"][:], func=AF.Ln,
                    bias=c_eps[:]),
                    r=[K("clb", kb, "v"), K("c", "eps")],
                    w=[K("clb", kb, "rstd")])
                cop("scalar", lambda e, clx=clx: e.activation(
                    out=clx["rstd"][:], in_=clx["rstd"][:], func=AF.Exp,
                    scale=-0.5), r=[K("clb", kb, "rstd")],
                    w=[K("clb", kb, "rstd")])
                cop("vector", lambda e, clx=clx: e.tensor_scalar(
                    out=clx["negmr"][:], in0=clx["m"][:],
                    scalar1=clx["rstd"][:], scalar2=-1.0, op0=ALU.mult,
                    op1=ALU.mult),
                    r=[K("clb", kb, "m"), K("clb", kb, "rstd")],
                    w=[K("clb", kb, "negmr")])
                cop("scalar", lambda e, t_xt=t_xt, t_out=t_out, clx=clx:
                    e.activation(
                    out=t_out[:], in_=t_xt[:], func=AF.Identity,
                    scale=clx["rstd"][:], bias=clx["negmr"][:]),
                    r=[keyx, K("clb", kb, "rstd"), K("clb", kb, "negmr")],
                    w=[keyout])

            def issue_xnT(grp):
                gb = grp % 2
                for k in range(NK):
                    p.dmat("sync", sXT[gb], xnT[gb][:, k * 512:(k + 1) * 512],
                           xnbuf[NWARM + grp * 512:NWARM + (grp + 1) * 512,
                                 k * 128:(k + 1) * 128],
                           r=[K("xnbuf", (NWARM // SUB) + grp * 4 + j)
                              for j in range(4)],
                           w=[K("xnT", gb, k)])

            issue_xnT(0)
            for sub in range(NSUB):
                buf = sub % 2
                gb = (sub // 4) % 2
                if sub % 4 == 0 and sub // 4 + 1 < NSUB // 4:
                    issue_xnT(sub // 4 + 1)
                p.dma("gpsimd", sXB[buf], t_xb[buf][:],
                      x_in[(NWARM + sub * 128):(NWARM + (sub + 1) * 128), :],
                      w=[K("t_xb", buf)])
                s4 = sub % 4
                for n in range(NSL2):
                    pb = (sub * 4 + n) % 8
                    for k in range(NK):
                        copq("tensor", lambda e, k=k, s4=s4, n=n, pb=pb, gb=gb:
                             e.matmul(
                                 out=pg[pb][:],
                                 lhsT=xnT[gb][:, k * 512 + s4 * 128:
                                             k * 512 + (s4 + 1) * 128],
                                 rhs=wgset[:, k * DM + n * 512:
                                           k * DM + (n + 1) * 512],
                                 start=(k == 0), stop=False),
                             r=[K("xnT", gb, k), K("wgset", k)],
                             w=([K("pg", pb)] if k == 0 else []))
                    cop("tensor", lambda e, n=n, pb=pb: e.matmul(
                        out=pg[pb][:], lhsT=c_ones_row[:],
                        rhs=c_bgb[0:1, n * 512:(n + 1) * 512],
                        start=False, stop=True),
                        r=[K("c", 1), K("c", 10)] +
                          [K("xnT", gb, kk) for kk in range(NK)] +
                          [K("wgset", kk) for kk in range(NK)],
                        w=[K("pg", pb)])
                    cop("scalar", lambda e, n=n, pb=pb, buf=buf:
                        e.activation(
                        out=t_gate[buf][:, n * 512:(n + 1) * 512],
                        in_=pg[pb][:], func=AF.Sigmoid),
                        r=[K("pg", pb)], w=[K("gate", buf, n)])
                ln_b(t_xb[buf], t_sqb, clb[buf], K("t_xb", buf),
                     t_xn1[buf], K("t_xn1", buf), buf)
                cop("vector", lambda e, buf=buf: e.tensor_tensor(
                    out=t_yh[buf][:], in0=t_xn1[buf][:], in1=c_dp[:],
                    op=ALU.mult),
                    r=[K("t_xn1", buf), K("c", 9)], w=[K("t_yh", buf)])

                def yh_add(e, sub=sub, buf=buf):
                    y3 = y_all[:, sub * NH:(sub + 1) * NH].rearrange(
                        "p (h o) -> p h o", o=1)
                    t3 = t_yh[buf][:].rearrange("p (h d) -> p h d", d=64)
                    y3b, t3b = bass.broadcast_tensor_aps(y3, t3)
                    return e.tensor_tensor(out=t3, in0=t3b, in1=y3b, op=ALU.add)
                cop("vector", yh_add,
                    r=[K("t_yh", buf), K("yall", sub)], w=[K("t_yh", buf)])
                cop("vector", lambda e, buf=buf: e.tensor_tensor(
                    out=t_yfb[buf][:], in0=t_yh[buf][:], in1=t_gate[buf][:],
                    op=ALU.mult),
                    r=[K("t_yh", buf)] + [K("gate", buf, n)
                                          for n in range(NSL2)],
                    w=[K("t_yfb", buf)])
                p.dma("scalar", sYF[buf], yfbuf[sub * 128:(sub + 1) * 128, :],
                      t_yfb[buf][:], r=[K("t_yfb", buf)], w=[K("yfbuf", sub)])
            p.barrier()

        with ExitStack() as cb2:
            woset = sb("woset", (128, NK * DM), BF16, c=cb2)
            yfT = [sb(f"yfT{i}", (128, NK * 512), BF16, c=cb2)
                   for i in range(2)]
            t_s = [sb(f"t_s{i}", (128, DM), c=cb2) for i in range(2)]
            t_gr = [sb(f"t_gr{i}", (128, DM), c=cb2) for i in range(2)]
            t_xb2 = [sb(f"t_xb2{i}", (128, DM), c=cb2) for i in range(2)]
            t_o = [sb(f"t_o{i}", (128, DM), c=cb2) for i in range(2)]
            po = [ps(f"po{i}", (128, 512), c=cb2) for i in range(8)]
            sWO = sem("sWO")
            sYT = [sem("sYT0"), sem("sYT1")]
            sGR = [sem("sGR0"), sem("sGR1")]
            sXB2 = [sem("sXB20"), sem("sXB21")]
            sOUT = [sem("sOUT0"), sem("sOUT1")]
            for k in range(NK):
                p.dma("gpsimd", sWO, woset[:, k * DM:(k + 1) * DM],
                      wo_in[k * 128:(k + 1) * 128, :], w=[K("woset", k)])
            def issue_yfT(grp):
                gb = grp % 2
                for k in range(NK):
                    p.dmat("sync", sYT[gb], yfT[gb][:, k * 512:(k + 1) * 512],
                           yfbuf[grp * 512:(grp + 1) * 512,
                                 k * 128:(k + 1) * 128],
                           r=[K("yfbuf", grp * 4 + j) for j in range(4)],
                           w=[K("yfT", gb, k)])

            issue_yfT(0)
            for sub in range(NSUB):
                buf = sub % 2
                gb = (sub // 4) % 2
                if sub % 4 == 0 and sub // 4 + 1 < NSUB // 4:
                    issue_yfT(sub // 4 + 1)
                p.dma("gpsimd", sGR[buf], t_gr[buf][:],
                      gbuf[sub * 128:(sub + 1) * 128, :],
                      r=[K("gbuf", sub, n) for n in range(NSL2)],
                      w=[K("t_gr", buf)])
                p.dma("gpsimd", sXB2[buf], t_xb2[buf][:],
                      x_in[(NWARM + sub * 128):(NWARM + (sub + 1) * 128), :],
                      w=[K("t_xb2", buf)])
                s4 = sub % 4
                for n in range(NSL2):
                    pb = (sub * 4 + n) % 8
                    for k in range(NK):
                        fn = (lambda e, k=k, s4=s4, n=n, pb=pb, gb=gb:
                              e.matmul(
                                  out=po[pb][:],
                                  lhsT=yfT[gb][:, k * 512 + s4 * 128:
                                              k * 512 + (s4 + 1) * 128],
                                  rhs=woset[:, k * DM + n * 512:
                                            k * DM + (n + 1) * 512],
                                  start=(k == 0), stop=(k == NK - 1)))
                        if k < NK - 1:
                            copq("tensor", fn,
                                 r=[K("yfT", gb, k), K("woset", k)],
                                 w=([K("po", pb)] if k == 0 else []))
                        else:
                            cop("tensor", fn,
                                r=[K("yfT", gb, kk) for kk in range(NK)] +
                                  [K("woset", kk) for kk in range(NK)],
                                w=[K("po", pb)])
                    cop("scalar", lambda e, n=n, pb=pb, buf=buf:
                        e.activation(
                        out=t_s[buf][:, n * 512:(n + 1) * 512], in_=po[pb][:],
                        func=AF.Identity),
                        r=[K("po", pb)], w=[K("t_s", buf, n)])
                cop("vector", lambda e, buf=buf: e.tensor_tensor(
                    out=t_o[buf][:], in0=t_s[buf][:], in1=t_gr[buf][:],
                    op=ALU.mult),
                    r=[K("t_gr", buf)] + [K("t_s", buf, n)
                                          for n in range(NSL2)],
                    w=[K("t_o", buf)])
                cop("vector", lambda e, buf=buf: e.tensor_tensor(
                    out=t_o[buf][:], in0=t_o[buf][:], in1=t_xb2[buf][:],
                    op=ALU.add),
                    r=[K("t_o", buf), K("t_xb2", buf)], w=[K("t_o", buf)])
                p.dma("scalar", sOUT[buf], out_d[sub * 128:(sub + 1) * 128, :],
                      t_o[buf][:], r=[K("t_o", buf)], w=[K("outd", sub)])
            p.fence("gpsimd", [K("outd", sub) for sub in range(NSUB)])

    p.replay()
    return nc


def _shard_inputs(L0_attn_x, L0_attn_Wx, L0_attn_Wg, L0_attn_bg, L0_attn_Wo,
                  L0_attn_A, L0_attn_Dp, L0_ffn_W1, L0_ffn_W2):
    x = np.asarray(L0_attn_x, np.float32)
    wx = np.ascontiguousarray(np.asarray(L0_attn_Wx, np.float32))
    wg = np.ascontiguousarray(np.asarray(L0_attn_Wg, np.float32))
    wo = np.ascontiguousarray(np.asarray(L0_attn_Wo, np.float32))
    w1 = np.ascontiguousarray(np.asarray(L0_ffn_W1, np.float32))
    w2 = np.ascontiguousarray(np.asarray(L0_ffn_W2, np.float32))
    bg = np.ascontiguousarray(np.asarray(L0_attn_bg, np.float32).reshape(1, DM))
    at = np.ascontiguousarray(np.asarray(L0_attn_A, np.float32).T)
    dp = np.asarray(L0_attn_Dp, np.float32)
    dp_rep = np.ascontiguousarray(
        np.broadcast_to(np.repeat(dp, DM // NH)[None, :], (128, DM)))
    ones_row = np.ones((1, 128), ml_dtypes.bfloat16)
    ones_col = np.ones((128, 1), ml_dtypes.bfloat16)
    ones_row_f = np.ones((1, 128), np.float32)
    ones_col_f = np.ones((128, 1), np.float32)
    ident_b = np.eye(128, dtype=ml_dtypes.bfloat16)
    ident32_f = np.eye(32, dtype=np.float32)
    sel_h = np.zeros((NH, NH * 128), ml_dtypes.bfloat16)
    for h in range(NH):
        sel_h[h, h * 128:(h + 1) * 128] = 1.0

    in_maps = []
    spans = x.shape[1] // 4
    for k in range(8):
        bi, si = k // 4, k % 4
        lo = si * spans
        if si == 0:
            xpad = np.concatenate(
                [np.zeros((NWARM, DM), np.float32), x[bi, lo:lo + spans]],
                axis=0)
            woff = np.full((128, 1), -1e9, np.float32)
        else:
            xpad = np.ascontiguousarray(x[bi, lo - NWARM:lo + spans])
            woff = np.zeros((128, 1), np.float32)
        in_maps.append(dict(
            x=xpad, wx=wx, wg=wg, wo=wo, w1=w1, w2=w2, bg=bg, dp_rep=dp_rep,
            a_t=at, ones_row=ones_row, ones_col=ones_col,
            ones_row_f=ones_row_f, ones_col_f=ones_col_f, ident_b=ident_b,
            ident32_f=ident32_f, woff=woff, sel_h=sel_h))
    return in_maps


def kernel(**inputs):
    in_maps = _shard_inputs(**inputs)
    nc = bass.Bass("TRN2", target_bir_lowering=False)
    build_program(nc)
    res = run_bass_kernel_spmd(nc, in_maps, core_ids=list(range(8)))
    shards = [np.asarray(r["out"], np.float32) for r in res.results]
    out = np.empty((2, 16384, DM), np.float32)
    for k in range(8):
        bi, si = k // 4, k % 4
        out[bi, si * 4096:(si + 1) * 4096] = shards[k]
    return out
